# version 01
# baseline (speedup 1.0000x reference)
import os
import numpy as np
import concourse.bass as bass
import concourse.mybir as mybir

F32 = mybir.dt.float32
BF16 = mybir.dt.bfloat16
AF = mybir.ActivationFunctionType
ALU = mybir.AluOpType
AX = mybir.AxisListType

D = 2048
TT = 1024
KT = 16
CH = 2
CW = 512
IS = 4096
II = 1024
E = 8
NCORE = 8
EPS = 1e-5
NSLOT = 3

PHASES = ["ph1", "ph25", "ph3", "ph4", "ph5a", "ph5b", "ph5", "ph6", "full"]


class Sem:
    def __init__(self, h):
        self.h = h
        self.n = 0


class Mil:
    def __init__(self, eng, sem, cnt, async_=False):
        self.eng, self.sem, self.cnt, self.async_ = eng, sem, cnt, async_


class Builder:
    def __init__(self, nc):
        self.nc = nc
        self.q = {k: [] for k in ("tensor", "vector", "scalar", "gpsimd", "sync")}
        self.sems = {}
        self._semctx = []
        self.last_sp = {}

    def sem(self, name):
        if name not in self.sems:
            ctx = self.nc.semaphore(name)
            self.sems[name] = Sem(ctx.__enter__())
            self._semctx.append(ctx)
        return self.sems[name]

    def emit(self, eng, fn):
        self.q[eng].append(fn)

    def wait(self, eng, mil):
        if mil is None:
            return
        if mil.eng == eng and not mil.async_:
            return
        h, c = mil.sem.h, mil.cnt
        self.emit(eng, lambda e, h=h, c=c: e.wait_ge(h, c))

    def mark_t(self):
        s = self.sem("sPE")
        prev = s.n
        if prev >= 1:
            self.emit("tensor", lambda e, h=s.h, c=prev: e.wait_ge(h, c))
        s.n += 1
        cnt = s.n
        self.emit("tensor", lambda e, h=s.h: e.nop().then_inc(h, 1))
        return Mil("tensor", s, cnt)

    def op(self, eng, fn, sp=False):
        s = self.sem({"tensor": "sPE", "vector": "sDVE", "scalar": "sACT",
                      "gpsimd": "sGP"}[eng])
        if eng in ("vector", "scalar") and s.n >= 1:
            prev = s.n
            self.emit(eng, lambda e, h=s.h, c=prev: e.wait_ge(h, c))
        self.last_sp[eng] = sp
        s.n += 1
        cnt = s.n
        def run(e, fn=fn, h=s.h):
            fn(e).then_inc(h, 1)
        self.emit(eng, run)
        return Mil(eng, s, cnt)

    def v(self, fn, sp=False):
        return self.op("vector", fn, sp=sp)

    def a(self, fn, sp=False):
        return self.op("scalar", fn, sp=sp)

    def t(self, fn):
        return self.op("tensor", fn)

    def t_plain(self, fn):
        self.emit("tensor", lambda e, fn=fn: fn(e))

    def dma(self, queue, out_ap, in_ap, semname, **kw):
        s = self.sem(semname)
        s.n += 16
        cnt = s.n
        def run(e, o=out_ap, i=in_ap, h=s.h, kw=kw):
            e.dma_start(out=o, in_=i, **kw).then_inc(h, 16)
        self.emit(queue, run)
        return Mil(queue, s, cnt, async_=True)


def build_nc(stop="full", debug=False):
    nc = bass.Bass("TRN2", target_bir_lowering=False)
    B = Builder(nc)

    def din(name, shape):
        return nc.dram_tensor(name, shape, F32, kind="ExternalInput").ap()

    XT = din("XT", [D, TT])
    WIN = din("WIN", [D, D])
    WDEC = din("WDEC", [D, D])
    WGATE = din("WGATE", [D, D])
    WOUT = din("WOUT", [D, D])
    BIN = din("BIN", [128, KT])
    BDEC = din("BDEC", [128, KT])
    DBIAS = din("DBIAS", [128, KT])
    BGATE = din("BGATE", [128, KT])
    BOUT = din("BOUT", [128, KT])
    WRH = din("WRH", [128, KT * E])
    GWH = din("GWH", [128, KT])
    SWG = din("SWG", [D, IS])
    SWU = din("SWU", [D, IS])
    SWD = din("SWD", [IS, D])
    WGT = din("WGT", [E, D, II])
    WUT = din("WUT", [E, D, II])
    WDT = din("WDT", [E, II, D])
    IDENT = din("IDENT", [128, 128])
    ONESR = din("ONESR", [1, 128])
    ONESC = din("ONESC", [128, 1])
    BC8 = din("BC8", [E, E * 128])
    SELM = din("SELM", [128, 128])
    CSTS = din("CSTS", [128, 8])

    OUT = nc.dram_tensor("OUT", [D, TT], F32, kind="ExternalOutput").ap()

    cc_in = nc.dram_tensor("cc_in", [128, 32], F32, kind="Internal")
    cc_out = nc.dram_tensor("cc_out", [128 * NCORE, 32], F32, kind="Internal",
                            addr_space="Local")
    x1sp = nc.dram_tensor("x1sp", [128, KT * TT], F32, kind="Internal").ap()

    dbg = {}
    def dout(name, shape):
        return nc.dram_tensor(name, shape, F32, kind="ExternalOutput").ap()
    if debug:
        if stop == "ph1":
            dbg["SL"] = dout("DBG_SL", [128, KT * TT])
            dbg["PC"] = dout("DBG_PC", [128, KT * TT])
            dbg["CONTRIB"] = dout("DBG_CONTRIB", [128, 32])
        if stop == "ph25":
            dbg["Y"] = dout("DBG_Y", [128, KT * TT])
            dbg["CARRY"] = dout("DBG_CARRY", [128, KT])
            dbg["GAT"] = dout("DBG_GAT", [128, NCORE * 32])
        if stop == "ph3":
            dbg["X1"] = dout("DBG_X1", [128, KT * TT])
        if stop == "ph4":
            dbg["COMB"] = dout("DBG_COMB", [128, 64])
            dbg["GROW"] = dout("DBG_GROW", [1, TT])
            dbg["CBT"] = dout("DBG_CBT", [E, TT])
        if stop in ("ph5", "ph6"):
            dbg["RES"] = dout("DBG_RES", [128, KT * TT])
        if stop == "ph5a":
            dbg["HB"] = dout("DBG_HB", [128, 16 * TT])
        if stop == "ph5b":
            dbg["RES"] = dout("DBG_RES", [128, KT * TT])

    base = (nc.sbuf_base + 31) & ~31
    off = [base]

    def at(name, shape, dtype):
        h = nc.alloc_sbuf_tensor_at(name, shape, dtype, offset=off[0])
        sz = int(np.prod(shape[1:])) * mybir.dt.size(dtype)
        off[0] = (off[0] + sz + 31) & ~31
        return h

    ident = at("ident", [128, 128], F32)
    onesr = at("onesr", [1, 128], F32)
    onesc = at("onesc", [128, 1], F32)
    bc8 = at("bc8", [E, E * 128], BF16)
    selm = at("selm", [128, 128], F32)
    csts = at("csts", [128, 8], F32)
    b_in = at("b_in", [128, KT], F32)
    b_dec = at("b_dec", [128, KT], F32)
    b_db = at("b_db", [128, KT], F32)
    b_gate = at("b_gate", [128, KT], F32)
    b_out = at("b_out", [128, KT], F32)
    wrb = at("wrb", [128, KT * E], BF16)
    gwb = at("gwb", [128, KT], BF16)
    carry = at("carry", [128, KT], F32)
    tcar = at("tcar", [128, KT], F32)
    gat = at("gat", [128, NCORE * 32], F32)
    call = at("call", [128, 128], F32)
    selt = at("selt", [128, 128], F32)
    contrib = at("contrib", [128, 32], F32)
    comb = at("comb", [128, 64], F32)
    combT = at("combT", [E, TT], BF16)
    rows_nm = at("rows_nm", [1, TT], F32)
    rows_rs = at("rows_rs", [1, TT], F32)
    rows_tmp = at("rows_tmp", [1, TT], F32)
    cbc = [at("cbc0", [128, TT], BF16), at("cbc1", [128, TT], BF16)]
    gbc = at("gbc", [128, TT], BF16)
    rmx = at("rmx", [128, 1], F32)
    rm2 = at("rm2", [128, 1], F32)
    msk = at("msk", [128, 8], F32)
    mx8t = at("mx8t", [128, 8], F32)
    e1t = at("e1t", [128, 8], F32)
    dent = at("dent", [128, 1], F32)
    wsl = [at(f"wsl{i}", [128, 4096], BF16) for i in range(NSLOT)]

    U = off[0]
    xb = nc.alloc_sbuf_tensor_at("xb", [128, KT * TT], BF16, offset=U)
    Sl = nc.alloc_sbuf_tensor_at("Sl", [128, KT * TT], F32, offset=U + 32768)
    Pc = nc.alloc_sbuf_tensor_at("Pc", [128, KT * TT], BF16, offset=U + 98304)
    yv = Pc
    scr0 = U + 131072
    decay_t = nc.alloc_sbuf_tensor_at("decay_t", [128, TT], F32, offset=scr0)
    v_t = nc.alloc_sbuf_tensor_at("v_t", [128, TT], F32, offset=scr0 + 4096)
    zero_t = nc.alloc_sbuf_tensor_at("zero_t", [128, TT], F32, offset=scr0 + 8192)
    sg = [nc.alloc_sbuf_tensor_at("sg0", [128, CW], F32, offset=scr0 + 12288),
          nc.alloc_sbuf_tensor_at("sg1", [128, CW], F32, offset=scr0 + 14336)]
    gate_t = [nc.alloc_sbuf_tensor_at("gt0", [128, TT], BF16, offset=scr0 + 16384),
              nc.alloc_sbuf_tensor_at("gt1", [128, TT], BF16, offset=scr0 + 18432)]
    RES = nc.alloc_sbuf_tensor_at("RES", [128, KT * TT], F32, offset=U)
    x1b = nc.alloc_sbuf_tensor_at("x1b", [128, KT * TT], BF16, offset=U + 65536)
    hbig = nc.alloc_sbuf_tensor_at("hbig", [128, 16 * TT], BF16, offset=U + 98304)
    end = scr0 + 20480
    assert end <= nc.sbuf_top, f"SBUF overflow: end={end} top={nc.sbuf_top}"

    pm = [nc.alloc_psum_tensor(f"pm{i}", [128, CW], F32) for i in range(8)]
    pm_w = [None] * 8
    pm_r = [[] for _ in range(8)]

    def bank_open(bi):
        for m in pm_r[bi]:
            B.wait("tensor", m)
        pm_r[bi] = []

    def bank_read(bi, eng):
        B.wait(eng, pm_w[bi])

    def bank_mark_read(bi, mil):
        pm_r[bi].append(mil)

    def mm_group(bi, pairs, out_ap, start=True, stop=True):
        n = len(pairs)
        for i, (l, r) in enumerate(pairs):
            st = start and (i == 0)
            sp = stop and (i == n - 1)
            if sp:
                pm_w[bi] = B.t(lambda e, o=out_ap, l=l, r=r, st=st:
                               e.matmul(o, l, r, start=st, stop=True))
            else:
                B.t_plain(lambda e, o=out_ap, l=l, r=r, st=st:
                          e.matmul(o, l, r, start=st, stop=False))

    wstate = {"next": 0, "uses": [0] * NSLOT}
    sWF = [B.sem(f"sWF{i}") for i in range(NSLOT)]
    sWL = [B.sem(f"sWL{i}") for i in range(NSLOT)]

    def wload(dram2d, kt_tiles, cols, split=2, dram3d=None):
        i = wstate["next"]
        wstate["next"] += 1
        s = i % NSLOT
        use = wstate["uses"][s]
        wstate["uses"][s] += 1
        if use >= 1:
            B.emit("gpsimd", lambda e, h=sWF[s].h, c=use: e.wait_ge(h, c))
        sb3 = wsl[s].rearrange("p (k c) -> p k c", k=kt_tiles)
        dr3 = dram3d if dram3d is not None else dram2d.rearrange("(k p) c -> p k c", p=128)
        if wslot_view0[0] is None and kt_tiles == 16:
            wslot_view0[0] = sb3
        kstep = kt_tiles // split
        for part in range(split):
            sWL[s].n += 16
            o = sb3[:, part * kstep:(part + 1) * kstep, :]
            i_ = dr3[:, part * kstep:(part + 1) * kstep, :]
            def run(e, o=o, i_=i_, h=sWL[s].h):
                e.dma_start(out=o, in_=i_).then_inc(h, 16)
            B.emit("gpsimd", run)
        return s, sWL[s].n, sb3

    wslot_view0 = [None]

    def wready(slot, cnt):
        B.emit("tensor", lambda e, h=sWL[slot].h, c=cnt: e.wait_ge(h, c))

    def wfree(slot):
        spe = B.sem("sPE")
        if spe.n >= 1:
            B.emit("tensor", lambda e, h=spe.h, c=spe.n: e.wait_ge(h, c))
        sWF[slot].n += 1
        B.emit("tensor", lambda e, h=sWF[slot].h: e.nop().then_inc(h, 1))

    for dst, src in ((ident, IDENT), (onesr, ONESR), (onesc, ONESC), (selm, SELM), (csts, CSTS),
                     (b_in, BIN), (b_dec, BDEC), (b_db, DBIAS), (b_gate, BGATE),
                     (b_out, BOUT)):
        consts_mil = B.dma("sync", dst[:, :], src[:, :], "sY")
    B.dma("gpsimd", bc8[:, :], BC8[:, :], "sX")
    B.dma("gpsimd", wrb[:, :], WRH[:, :], "sX")
    smalls_mil = B.dma("gpsimd", gwb[:, :], GWH[:, :], "sX")
    xb3 = xb.rearrange("p (k t) -> p k t", k=KT)
    xt3 = XT.rearrange("(k p) t -> p k t", p=128)
    B.dma("gpsimd", xb3[:, 0:8, :], xt3[:, 0:8, :], "sX")
    xb_mil = B.dma("gpsimd", xb3[:, 8:16, :], xt3[:, 8:16, :], "sX")

    B.wait("vector", consts_mil)
    bdec_mil = B.v(lambda e: e.tensor_add(b_dec[:, :], b_dec[:, :], b_db[:, :]))
    zero_mil = B.v(lambda e: e.memzero(zero_t[:, :]), sp=True)

    bank_rr = [0]
    def next_bank(nb=8, bbase=0):
        bi = bbase + bank_rr[0] % nb
        bank_rr[0] += 1
        return bi

    def proj_mm(vw, jj, ch, bi, rhs3=None):
        r3 = rhs3 if rhs3 is not None else xb3
        pairs = [(vw[:, k, jj * 128:(jj + 1) * 128],
                  r3[:, k, ch * CW:(ch + 1) * CW]) for k in range(KT)]
        mm_group(bi, pairs, pm[bi][:, :])

    y_src_mil = [None] * KT
    scr_dec_free = None
    scr_sg_free = [None, None]
    first_x = [False]

    def wait_x_once():
        if not first_x[0]:
            B.wait("tensor", xb_mil)
            first_x[0] = True

    contrib_mil = None
    for pb in range(8):
        sd, cd, vd = wload(WDEC[:, pb * 256:(pb + 1) * 256], KT, 256)
        si, ci, vi = wload(WIN[:, pb * 256:(pb + 1) * 256], KT, 256)
        for jj in range(2):
            j = pb * 2 + jj
            dmils = []
            for ch in range(CH):
                bi = next_bank()
                bank_open(bi)
                if jj == 0 and ch == 0:
                    wready(sd, cd)
                    wait_x_once()
                proj_mm(vd, jj, ch, bi)
                bank_read(bi, "scalar")
                if scr_dec_free is not None:
                    B.wait("scalar", scr_dec_free)
                if j == 0:
                    B.wait("scalar", bdec_mil)
                m = B.a(lambda e, o=decay_t[:, ch * CW:(ch + 1) * CW], i=pm[bi][:, :],
                        bi_=b_dec[:, j:j + 1]:
                        e.activation(o, i, AF.Sigmoid, bias=bi_))
                bank_mark_read(bi, m)
                dmils.append(m)
            if jj == 1:
                wfree(sd)
            for ch in range(CH):
                bi = next_bank()
                bank_open(bi)
                if jj == 0 and ch == 0:
                    wready(si, ci)
                proj_mm(vi, jj, ch, bi)
                bank_read(bi, "scalar")
                if scr_sg_free[ch] is not None:
                    B.wait("scalar", scr_sg_free[ch])
                m1 = B.a(lambda e, o=sg[ch][:, :], i=pm[bi][:, :], bi_=b_in[:, j:j + 1]:
                         e.activation(o, i, AF.Sigmoid, bias=bi_))
                bank_read(bi, "vector")
                B.wait("vector", m1)
                m2 = B.v(lambda e, o=v_t[:, ch * CW:(ch + 1) * CW], i=pm[bi][:, :],
                         bi_=b_in[:, j:j + 1], s_=sg[ch][:, :]:
                         e.scalar_tensor_tensor(o, i, bi_, s_, ALU.add, ALU.mult))
                bank_mark_read(bi, m2)
                B.wait("vector", dmils[ch])
                B.v(lambda e, o=sg[ch][:, :], d=decay_t[:, ch * CW:(ch + 1) * CW],
                    v=v_t[:, ch * CW:(ch + 1) * CW]:
                    e.tensor_mul(o, d, v))
                m4 = B.v(lambda e, o=v_t[:, ch * CW:(ch + 1) * CW],
                         a_=v_t[:, ch * CW:(ch + 1) * CW], b_=sg[ch][:, :]:
                         e.tensor_sub(o, a_, b_))
                scr_sg_free[ch] = m4
            if jj == 1:
                wfree(si)
            if j == 0:
                B.wait("vector", zero_mil)
            B.v(lambda e, o=Sl[:, j * TT:(j + 1) * TT], d=decay_t[:, :], v=v_t[:, :]:
                e.tensor_tensor_scan(o, d, v, 0.0, ALU.mult, ALU.add), sp=True)
            mpc = B.v(lambda e, o=Pc[:, j * TT:(j + 1) * TT], d=decay_t[:, :],
                      z=zero_t[:, :]:
                      e.tensor_tensor_scan(o, d, z, 1.0, ALU.mult, ALU.add), sp=True)
            B.v(lambda e, o=contrib[:, j:j + 1], i=Sl[:, (j + 1) * TT - 1:(j + 1) * TT]:
                e.tensor_copy(o, i))
            contrib_mil = B.v(
                lambda e, o=contrib[:, KT + j:KT + j + 1],
                i=Pc[:, (j + 1) * TT - 1:(j + 1) * TT]:
                e.tensor_copy(o, i))
            scr_dec_free = mpc

    def finish():
        sOUT = B.sems.get("sOUT")
        if sOUT is not None:
            B.emit("sync", lambda e, h=sOUT.h, c=sOUT.n: e.wait_ge(h, c))
        with nc.Block() as blk:
            blk.tensor(lambda e: [f(e) for f in B.q["tensor"]])
            blk.vector(lambda e: [f(e) for f in B.q["vector"]])
            blk.scalar(lambda e: [f(e) for f in B.q["scalar"]])
            blk.gpsimd(lambda e: [f(e) for f in B.q["gpsimd"]])
            blk.sync(lambda e: [f(e) for f in B.q["sync"]])
        return nc

    if stop == "ph1":
        B.wait("sync", contrib_mil)
        B.dma("sync", dbg["SL"][:, :], Sl[:, :], "sOUT")
        B.dma("sync", dbg["PC"][:, :], Pc.bitcast(BF16)[:, :], "sOUT") if False else None
        mcp = B.v(lambda e: e.tensor_copy(v_t[:, :], Pc[:, 0:TT]))
        B.wait("sync", mcp)
        B.dma("sync", dbg["PC"][:, 0:TT], v_t[:, :], "sOUT")
        B.dma("sync", dbg["CONTRIB"][:, :], contrib[:, :], "sOUT")
        return finish()

    B.wait("sync", contrib_mil)
    m_ccin = B.dma("sync", cc_in.ap()[:, :], contrib[:, :], "sY")
    sCC = B.sem("sCC")
    B.wait("gpsimd", m_ccin)
    sCC.n += 1
    def run_cc(e, h=sCC.h):
        e.collective_compute(
            "AllGather", ALU.bypass,
            replica_groups=[list(range(NCORE))],
            ins=[cc_in.ap().opt()],
            outs=[cc_out.ap().opt()],
        ).then_inc(h, 1)
    B.emit("gpsimd", run_cc)
    cc_mil = Mil("gpsimd", sCC, sCC.n, async_=True)
    B.wait("gpsimd", cc_mil)
    for r in range(NCORE):
        gat_m = B.dma("gpsimd", gat[:, r * 32:(r + 1) * 32],
                      cc_out.ap()[r * 128:(r + 1) * 128, :], "sX")
    B.wait("vector", gat_m)
    B.v(lambda e: e.memzero(call[:, :]), sp=True)
    for b in range(2):
        c0 = 4 * b
        def gF(c):
            return gat[:, c * 32:c * 32 + KT]
        def gP(c):
            return gat[:, c * 32 + KT:c * 32 + 2 * KT]
        def cnd(qq):
            i = c0 + qq
            return call[:, i * KT:(i + 1) * KT]
        B.v(lambda e, o=cnd(1), i=gF(c0 + 0): e.tensor_copy(o, i))
        B.v(lambda e, o=tcar[:, :], a_=gP(c0 + 1), b_=cnd(1): e.tensor_mul(o, a_, b_))
        B.v(lambda e, o=cnd(2), a_=gF(c0 + 1), b_=tcar[:, :]: e.tensor_add(o, a_, b_))
        B.v(lambda e, o=tcar[:, :], a_=gP(c0 + 2), b_=cnd(2): e.tensor_mul(o, a_, b_))
        B.v(lambda e, o=cnd(3), a_=gF(c0 + 2), b_=tcar[:, :]: e.tensor_add(o, a_, b_))
    B.v(lambda e: e.tensor_mul(selt[:, :], call[:, :], selm[:, :]))
    B.v(lambda e: e.tensor_add(selt[:, 0:64], selt[:, 0:64], selt[:, 64:128]))
    B.v(lambda e: e.tensor_add(selt[:, 0:32], selt[:, 0:32], selt[:, 32:64]))
    carry_mil = B.v(lambda e: e.tensor_add(carry[:, :], selt[:, 0:KT], selt[:, KT:2 * KT]))

    y_mil = [None] * KT
    gate_free = [None, None]
    for pb in range(8):
        sgt, cg, vg = wload(WGATE[:, pb * 256:(pb + 1) * 256], KT, 256)
        for jj in range(2):
            j = pb * 2 + jj
            gmils = []
            for ch in range(CH):
                bi = next_bank()
                bank_open(bi)
                if jj == 0 and ch == 0:
                    wready(sgt, cg)
                proj_mm(vg, jj, ch, bi)
                bank_read(bi, "scalar")
                if gate_free[j % 2] is not None:
                    B.wait("scalar", gate_free[j % 2])
                m = B.a(lambda e, o=gate_t[j % 2][:, ch * CW:(ch + 1) * CW],
                        i=pm[bi][:, :], bi_=b_gate[:, j:j + 1]:
                        e.activation(o, i, AF.Sigmoid, bias=bi_))
                bank_mark_read(bi, m)
                gmils.append(m)
            if jj == 1:
                wfree(sgt)
            B.wait("vector", gmils[1])
            B.v(lambda e, o=v_t[:, :], i=Pc[:, j * TT:(j + 1) * TT], s=carry[:, j:j + 1]:
                e.tensor_scalar(o, i, s, None, ALU.mult))
            B.v(lambda e, o=v_t[:, :], a_=v_t[:, :], b_=Sl[:, j * TT:(j + 1) * TT]:
                e.tensor_add(o, a_, b_))
            my = B.v(lambda e, o=yv[:, j * TT:(j + 1) * TT], a_=v_t[:, :],
                     b_=gate_t[j % 2][:, :]:
                     e.tensor_mul(o, a_, b_))
            y_mil[j] = my
            gate_free[j % 2] = my

    if stop == "ph25":
        mcp = B.v(lambda e: e.tensor_copy(Sl[:, :], yv[:, :]))
        B.wait("sync", mcp)
        B.dma("sync", dbg["Y"][:, :], Sl[:, :], "sOUT")
        B.dma("sync", dbg["CARRY"][:, :], carry[:, :], "sOUT")
        B.dma("sync", dbg["GAT"][:, :], gat[:, :], "sOUT")
        return finish()

    B.wait("sync", y_mil[KT - 1])
    res3 = RES.rearrange("p (k t) -> p k t", k=KT)
    m_pre0 = B.dma("sync", res3[:, 0:8, :], xt3[:, 0:8, :], "sY")
    m_pre1 = B.dma("sync", res3[:, 8:16, :], xt3[:, 8:16, :], "sY")
    yb3 = yv.rearrange("p (k t) -> p k t", k=KT)
    sq_free = [None, None]
    y_waited = [False]
    stats_opened = [False]
    last_sx = [None, None]
    last_sxx = [None, None]
    for pb in range(8):
        so, co, vo = wload(WOUT[:, pb * 256:(pb + 1) * 256], KT, 256)
        for jj in range(2):
            j = pb * 2 + jj
            for ch in range(CH):
                bi = next_bank(4)
                bank_open(bi)
                if jj == 0 and ch == 0:
                    wready(so, co)
                if not y_waited[0]:
                    B.wait("tensor", y_mil[KT - 1])
                    B.wait("tensor", consts_mil)
                    y_waited[0] = True
                proj_mm(vo, jj, ch, bi, rhs3=yb3)
                bank_read(bi, "vector")
                if j < 8:
                    B.wait("vector", m_pre0)
                elif j == 8 and ch == 0:
                    B.wait("vector", m_pre1)
                rsl = RES[:, j * TT + ch * CW: j * TT + (ch + 1) * CW]
                mev = B.v(lambda e, o=rsl, i=pm[bi][:, :], b_=b_out[:, j:j + 1], r=rsl:
                          e.scalar_tensor_tensor(o, i, b_, r, ALU.add, ALU.add))
                bank_mark_read(bi, mev)
                B.wait("scalar", mev)
                if sq_free[ch] is not None:
                    B.wait("scalar", sq_free[ch])
                msq = B.a(lambda e, o=sg[ch][:, :], i=rsl:
                          e.activation(o, i, AF.Square))
                if not stats_opened[0]:
                    for sbi in (4, 5, 6, 7):
                        bank_open(sbi)
                    stats_opened[0] = True
                B.wait("tensor", mev)
                mS = B.t(lambda e, o=pm[4 + ch][0:1, :], l=onesc[:, :], r=rsl,
                         st=(j == 0), sp=(j == KT - 1):
                         e.matmul(o, l, r, start=st, stop=sp))
                last_sx[ch] = mS
                B.wait("tensor", msq)
                mQ = B.t(lambda e, o=pm[6 + ch][0:1, :], l=onesc[:, :], r=sg[ch][:, :],
                         st=(j == 0), sp=(j == KT - 1):
                         e.matmul(o, l, r, start=st, stop=sp))
                last_sxx[ch] = mQ
                sq_free[ch] = mQ
        wfree(so)
    for ch in range(CH):
        pm_w[4 + ch] = last_sx[ch]
        pm_w[6 + ch] = last_sxx[ch]

    rows_tensor_read = [None]

    def ln_rows_and_bcast():
        mvar = None
        for ch in range(CH):
            bank_read(4 + ch, "vector")
            bank_read(6 + ch, "vector")
            if rows_tensor_read[0] is not None:
                B.wait("vector", rows_tensor_read[0])
                rows_tensor_read[0] = None
            mnm = B.v(lambda e, o=rows_nm[0:1, ch * CW:(ch + 1) * CW],
                      i=pm[4 + ch][0:1, :]:
                      e.tensor_scalar(o, i, -1.0 / D, None, ALU.mult))
            B.v(lambda e, o=rows_tmp[0:1, ch * CW:(ch + 1) * CW],
                a_=rows_nm[0:1, ch * CW:(ch + 1) * CW],
                b_=rows_nm[0:1, ch * CW:(ch + 1) * CW]:
                e.tensor_mul(o, a_, b_))
            B.v(lambda e, o=rows_tmp[0:1, ch * CW:(ch + 1) * CW],
                i=rows_tmp[0:1, ch * CW:(ch + 1) * CW]:
                e.tensor_scalar(o, i, -EPS, None, ALU.add))
            mvar = B.v(lambda e, o=rows_rs[0:1, ch * CW:(ch + 1) * CW],
                       i=pm[6 + ch][0:1, :],
                       t_=rows_tmp[0:1, ch * CW:(ch + 1) * CW]:
                       e.scalar_tensor_tensor(o, i, 1.0 / D, t_, ALU.mult, ALU.subtract))
            bank_mark_read(4 + ch, mnm)
            bank_mark_read(6 + ch, mvar)
        B.wait("scalar", mvar)
        B.a(lambda e: e.activation(rows_tmp[0:1, :], rows_rs[0:1, :], AF.Ln), sp=True)
        mrs = B.a(lambda e: e.activation(rows_rs[0:1, :], rows_tmp[0:1, :], AF.Exp,
                                         scale=csts[0:1, 1:2]), sp=True)
        B.wait("tensor", mrs)
        nm_bi, rs_bi = [0, 1], [2, 3]
        lastb = None
        for ch in range(CH):
            for bi, row in ((nm_bi[ch], rows_nm), (rs_bi[ch], rows_rs)):
                bank_open(bi)
                pm_w[bi] = B.t(lambda e, o=pm[bi][:, :],
                               r=row[0:1, ch * CW:(ch + 1) * CW]:
                               e.matmul(o, onesr[0:1, :], r, start=True, stop=True))
                lastb = pm_w[bi]
        rows_tensor_read[0] = lastb
        return nm_bi, rs_bi

    nm_bi, rs_bi = ln_rows_and_bcast()
    x1_mil = [None] * KT
    spill_mil = [None] * KT
    for j in range(KT):
        m3 = None
        for ch in range(CH):
            rsl = RES[:, j * TT + ch * CW: j * TT + (ch + 1) * CW]
            xsl = x1b[:, j * TT + ch * CW: j * TT + (ch + 1) * CW]
            bank_read(nm_bi[ch], "vector")
            bank_read(rs_bi[ch], "vector")
            m1 = B.v(lambda e, o=rsl, a_=rsl, b_=pm[nm_bi[ch]][:, :]:
                     e.tensor_add(o, a_, b_))
            B.v(lambda e, o=xsl, a_=rsl, b_=pm[rs_bi[ch]][:, :]:
                e.tensor_mul(o, a_, b_))
            m3 = B.v(lambda e, o=rsl, a_=rsl, b_=pm[rs_bi[ch]][:, :]:
                     e.tensor_mul(o, a_, b_))
            bank_mark_read(nm_bi[ch], m1)
            bank_mark_read(rs_bi[ch], m3)
        x1_mil[j] = m3
        B.wait("sync", m3)
        spill_mil[j] = B.dma("sync", x1sp[:, j * TT:(j + 1) * TT],
                             RES[:, j * TT:(j + 1) * TT], "sY")

    if stop == "ph3":
        B.dma("sync", dbg["X1"][:, :], RES[:, :], "sOUT")
        return finish()

    x1b3 = x1b.rearrange("p (k t) -> p k t", k=KT)
    comb_mil = [None] * 8
    first_r = [False]
    for tg in range(8):
        bi = next_bank(4)
        bank_open(bi)
        if not first_r[0]:
            B.wait("tensor", smalls_mil)
            B.wait("tensor", x1_mil[KT - 1])
            first_r[0] = True
        pairs = [(x1b3[:, k, tg * 128:(tg + 1) * 128], wrb[:, k * 8:(k + 1) * 8])
                 for k in range(KT)]
        mm_group(bi, pairs, pm[bi][:, 0:8])
        p8 = pm[bi][:, 0:8]
        bank_read(bi, "vector")
        mlg = B.v(lambda e, o=msk[:, :], i=p8: e.tensor_copy(o, i))
        bank_mark_read(bi, mlg)
        B.v(lambda e, o=mx8t[:, :], i=msk[:, :]: e.max(out=o, in_=i), sp=True)
        B.v(lambda e, o=mx8t[:, 2:8]: e.memset(o, -1e30), sp=True)
        B.v(lambda e, o=e1t[:, :], m=mx8t[:, :], i=msk[:, :]:
            e.match_replace(out=o, in_to_replace=m, in_values=i, imm_value=-1e30), sp=True)
        mneg = B.v(lambda e, o=rmx[:, :], i=mx8t[:, 0:1]: e.tensor_scalar_mul(o, i, -1.0))
        B.wait("scalar", mneg)
        B.a(lambda e, o=msk[:, :], i=msk[:, :]:
            e.activation(o, i, AF.Exp, bias=rmx[:, 0:1]), sp=True)
        mez = B.a(lambda e, o=e1t[:, :], i=e1t[:, :]:
                  e.activation(o, i, AF.Exp, bias=rmx[:, 0:1]), sp=True)
        B.wait("vector", mez)
        B.v(lambda e, o=e1t[:, :], a_=msk[:, :], b_=e1t[:, :]: e.tensor_sub(o, a_, b_))
        B.v(lambda e, o=dent[:, :], i=e1t[:, :]: e.reduce_sum(o, i, axis=AX.X), sp=True)
        B.v(lambda e, o=rm2[:, :], i=dent[:, :]: e.reciprocal(o, i), sp=True)
        mc = B.v(lambda e, o=comb[:, tg * 8:(tg + 1) * 8], i=e1t[:, :]:
                 e.tensor_scalar(o, i, rm2[:, 0:1], None, ALU.mult))
        comb_mil[tg] = mc
    combT_mil = None
    for half in range(2):
        for tgi in range(4):
            tg = half * 4 + tgi
            bi = 4 + tgi
            bank_open(bi)
            B.wait("tensor", comb_mil[tg])
            pm_w[bi] = B.t(lambda e, o=pm[bi][0:8, 0:128],
                           i=comb[:, tg * 8:(tg + 1) * 8]:
                           e.transpose(o, i, ident[:, :]))
            bank_read(bi, "vector")
            mct = B.v(lambda e, o=combT[0:8, tg * 128:(tg + 1) * 128],
                      i=pm[bi][0:8, 0:128]:
                      e.tensor_copy(o, i))
            bank_mark_read(bi, mct)
            combT_mil = mct
    mexr = None
    for ch in range(CH):
        bi = 6 + ch
        bank_open(bi)
        pairs = [(gwb[:, k:k + 1], x1b3[:, k, ch * CW:(ch + 1) * CW]) for k in range(KT)]
        mm_group(bi, pairs, pm[bi][0:1, :])
        bank_read(bi, "scalar")
        mexr = B.a(lambda e, o=rows_tmp[0:1, ch * CW:(ch + 1) * CW], i=pm[bi][0:1, :]:
                   e.activation(o, i, AF.Exp, scale=csts[0:1, 2:3]))
        bank_mark_read(bi, mexr)
    B.wait("vector", mexr)
    if rows_tensor_read[0] is not None:
        B.wait("vector", rows_tensor_read[0])
        rows_tensor_read[0] = None
    B.v(lambda e: e.tensor_scalar(rows_nm[0:1, :], rows_tmp[0:1, :], 1.0, None, ALU.add))
    grow_mil = B.v(lambda e: e.reciprocal(rows_rs[0:1, :], rows_nm[0:1, :]), sp=True)
    B.wait("tensor", grow_mil)
    gbc_mil = None
    for ch in range(CH):
        bi = 4 + ch
        bank_open(bi)
        pm_w[bi] = B.t(lambda e, o=pm[bi][:, :], r=rows_rs[0:1, ch * CW:(ch + 1) * CW]:
                       e.matmul(o, onesr[0:1, :], r, start=True, stop=True))
        rows_tensor_read[0] = pm_w[bi]
        bank_read(bi, "vector")
        mg = B.v(lambda e, o=gbc[:, ch * CW:(ch + 1) * CW], i=pm[bi][:, :]:
                 e.tensor_copy(o, i))
        bank_mark_read(bi, mg)
        gbc_mil = mg

    if stop == "ph4":
        B.wait("sync", gbc_mil)
        B.dma("sync", dbg["COMB"][:, :], comb[:, :], "sOUT")
        B.dma("sync", dbg["GROW"][0:1, :], rows_rs[0:1, :], "sOUT")
        B.wait("gpsimd", combT_mil)
        B.dma("gpsimd", dbg["CBT"][:, :], combT[:, :], "sOUT")
        return finish()

    hb3 = hbig.rearrange("p (k t) -> p k t", k=16)
    uscr = gate_t
    uscr_free = [None, None]
    spill_all = spill_mil[KT - 1]
    res_sh_mil = [[None] * CH for _ in range(KT)]
    down_done_mil = [None]
    for H in range(2):
        h_last = None
        for pb in range(8):
            sgp, cgp, vgp = wload(SWG[:, H * 2048 + pb * 256: H * 2048 + (pb + 1) * 256],
                                  KT, 256)
            sup, cup, vup = wload(SWU[:, H * 2048 + pb * 256: H * 2048 + (pb + 1) * 256],
                                  KT, 256)
            for jj in range(2):
                it = pb * 2 + jj
                umils = []
                for ch in range(CH):
                    bi = next_bank()
                    bank_open(bi)
                    if jj == 0 and ch == 0:
                        wready(sgp, cgp)
                    proj_mm(vgp, jj, ch, bi, rhs3=x1b3)
                    bank_read(bi, "scalar")
                    if down_done_mil[0] is not None:
                        B.wait("scalar", down_done_mil[0])
                    mgs = B.a(lambda e, o=hb3[:, it, ch * CW:(ch + 1) * CW],
                              i=pm[bi][:, :]:
                              e.activation(o, i, AF.Silu))
                    bank_mark_read(bi, mgs)
                    umils.append(mgs)
                if jj == 1:
                    wfree(sgp)
                for ch in range(CH):
                    bi = next_bank()
                    bank_open(bi)
                    if jj == 0 and ch == 0:
                        wready(sup, cup)
                    proj_mm(vup, jj, ch, bi, rhs3=x1b3)
                    bank_read(bi, "scalar")
                    if uscr_free[it % 2] is not None:
                        B.wait("scalar", uscr_free[it % 2])
                    ucp = B.a(lambda e, o=uscr[it % 2][:, ch * CW:(ch + 1) * CW],
                              i=pm[bi][:, :]:
                              e.copy(o, i))
                    bank_mark_read(bi, ucp)
                    B.wait("vector", umils[ch])
                    B.wait("vector", ucp)
                    if down_done_mil[0] is not None:
                        B.wait("vector", down_done_mil[0])
                    mh = B.v(lambda e, o=hb3[:, it, ch * CW:(ch + 1) * CW],
                             a_=hb3[:, it, ch * CW:(ch + 1) * CW],
                             b_=uscr[it % 2][:, ch * CW:(ch + 1) * CW]:
                             e.tensor_mul(o, a_, b_))
                    h_last = mh
                    uscr_free[it % 2] = mh
                if jj == 1:
                    wfree(sup)
        if stop == "ph5a" and H == 0:
            B.wait("gpsimd", h_last)
            B.dma("gpsimd", dbg["HB"][:, :], hbig[:, :], "sOUT")
            return finish()
        down_done_mil[0] = None
        h_all = h_last
        SWD3 = SWD.rearrange("(k p) c -> p k c", p=128)
        for pb in range(8):
            sdp, cdp, vdp = wload(None, KT, 256,
                                  dram3d=SWD3[:, H * 16:(H + 1) * 16,
                                              pb * 256:(pb + 1) * 256])
            for jj in range(2):
                j = pb * 2 + jj
                for ch in range(CH):
                    bi = next_bank()
                    bank_open(bi)
                    if jj == 0 and ch == 0:
                        wready(sdp, cdp)
                        B.wait("tensor", h_all)
                    pairs = [(vdp[:, k, jj * 128:(jj + 1) * 128],
                              hb3[:, k, ch * CW:(ch + 1) * CW]) for k in range(KT)]
                    mm_group(bi, pairs, pm[bi][:, :])
                    rsl = RES[:, j * TT + ch * CW: j * TT + (ch + 1) * CW]
                    if H == 0:
                        bank_read(bi, "scalar")
                        if j == 0 and ch == 0:
                            B.wait("scalar", spill_all)
                        mev = B.a(lambda e, o=rsl, i=pm[bi][:, :]: e.copy(o, i))
                    else:
                        bank_read(bi, "vector")
                        B.v(lambda e, o=rsl, a_=rsl, b_=pm[bi][:, :]:
                            e.tensor_add(o, a_, b_))
                        mev = B.v(lambda e, o=rsl, a_=rsl,
                                  b_=gbc[:, ch * CW:(ch + 1) * CW]:
                                  e.tensor_mul(o, a_, b_))
                    bank_mark_read(bi, mev)
                    res_sh_mil[j][ch] = mev
            wfree(sdp)
        down_done_mil[0] = B.mark_t()
        if stop == "ph5b" and H == 0:
            B.wait("sync", res_sh_mil[KT - 1][CH - 1])
            B.dma("sync", dbg["RES"][:, :], RES[:, :], "sOUT")
            return finish()

    if stop == "ph5":
        B.wait("sync", res_sh_mil[KT - 1][CH - 1])
        B.dma("sync", dbg["RES"][:, :], RES[:, :], "sOUT")
        return finish()

    res_ex_mil = [[None] * CH for _ in range(KT)]
    for grp in range(4):
        e0, e1 = 2 * grp, 2 * grp + 1
        cbc_last = None
        for ei, ex in enumerate((e0, e1)):
            for ch in range(CH):
                bi = next_bank()
                bank_open(bi)
                if grp == 0 and ei == 0 and ch == 0:
                    B.wait("tensor", combT_mil)
                pm_w[bi] = B.t(lambda e, o=pm[bi][:, :],
                               l=bc8[:, ex * 128:(ex + 1) * 128],
                               r=combT[0:8, ch * CW:(ch + 1) * CW]:
                               e.matmul(o, l, r, start=True, stop=True))
                bank_read(bi, "vector")
                mcb = B.v(lambda e, o=cbc[ei][:, ch * CW:(ch + 1) * CW], i=pm[bi][:, :]:
                          e.tensor_copy(o, i))
                bank_mark_read(bi, mcb)
                cbc_last = mcb
        for ei, ex in enumerate((e0, e1)):
            for pb in range(4):
                sgp, cgp, vgp = wload(WGT[ex][:, pb * 256:(pb + 1) * 256], KT, 256)
                sup, cup, vup = wload(WUT[ex][:, pb * 256:(pb + 1) * 256], KT, 256)
                for jj in range(2):
                    it = ei * 8 + pb * 2 + jj
                    umils = []
                    for ch in range(CH):
                        bi = next_bank()
                        bank_open(bi)
                        if jj == 0 and ch == 0:
                            wready(sgp, cgp)
                        proj_mm(vgp, jj, ch, bi, rhs3=x1b3)
                        bank_read(bi, "scalar")
                        if down_done_mil[0] is not None:
                            B.wait("scalar", down_done_mil[0])
                        mgs = B.a(lambda e, o=hb3[:, it, ch * CW:(ch + 1) * CW],
                                  i=pm[bi][:, :]:
                                  e.activation(o, i, AF.Silu))
                        bank_mark_read(bi, mgs)
                        umils.append(mgs)
                    if jj == 1:
                        wfree(sgp)
                    for ch in range(CH):
                        bi = next_bank()
                        bank_open(bi)
                        if jj == 0 and ch == 0:
                            wready(sup, cup)
                        proj_mm(vup, jj, ch, bi, rhs3=x1b3)
                        bank_read(bi, "scalar")
                        if uscr_free[it % 2] is not None:
                            B.wait("scalar", uscr_free[it % 2])
                        ucp = B.a(lambda e, o=uscr[it % 2][:, ch * CW:(ch + 1) * CW],
                                  i=pm[bi][:, :]:
                                  e.copy(o, i))
                        bank_mark_read(bi, ucp)
                        B.wait("vector", umils[ch])
                        B.wait("vector", ucp)
                        if down_done_mil[0] is not None:
                            B.wait("vector", down_done_mil[0])
                        B.wait("vector", cbc_last)
                        B.v(lambda e, o=hb3[:, it, ch * CW:(ch + 1) * CW],
                            a_=hb3[:, it, ch * CW:(ch + 1) * CW],
                            b_=cbc[ei][:, ch * CW:(ch + 1) * CW]:
                            e.tensor_mul(o, a_, b_))
                        mh = B.v(lambda e, o=hb3[:, it, ch * CW:(ch + 1) * CW],
                                 a_=hb3[:, it, ch * CW:(ch + 1) * CW],
                                 b_=uscr[it % 2][:, ch * CW:(ch + 1) * CW]:
                                 e.tensor_mul(o, a_, b_))
                        hc_all = mh
                        uscr_free[it % 2] = mh
                    if jj == 1:
                        wfree(sup)
        down_done_mil[0] = None
        for hb in range(4):
            sp0, cp0, vp0 = wload(WDT[e0][:, hb * 512:(hb + 1) * 512], 8, 512)
            sp1, cp1, vp1 = wload(WDT[e1][:, hb * 512:(hb + 1) * 512], 8, 512)
            for hti in range(4):
                for ch in range(CH):
                    bi = hti * 2 + ch
                    bank_open(bi)
                    if hti == 0 and ch == 0:
                        wready(sp0, cp0)
                        B.wait("tensor", hc_all)
                    pairs = [(vp0[:, k, hti * 128:(hti + 1) * 128],
                              hb3[:, k, ch * CW:(ch + 1) * CW]) for k in range(8)]
                    mm_group(bi, pairs, pm[bi][:, :], start=True, stop=False)
            wfree(sp0)
            for hti in range(4):
                j = hb * 4 + hti
                for ch in range(CH):
                    bi = hti * 2 + ch
                    if hti == 0 and ch == 0:
                        wready(sp1, cp1)
                    pairs = [(vp1[:, k, hti * 128:(hti + 1) * 128],
                              hb3[:, 8 + k, ch * CW:(ch + 1) * CW]) for k in range(8)]
                    mm_group(bi, pairs, pm[bi][:, :], start=False, stop=True)
                    bank_read(bi, "vector")
                    rsl = RES[:, j * TT + ch * CW: j * TT + (ch + 1) * CW]
                    if grp == 0:
                        B.wait("vector", res_sh_mil[j][ch])
                    mev = B.v(lambda e, o=rsl, a_=rsl, b_=pm[bi][:, :]:
                              e.tensor_add(o, a_, b_))
                    bank_mark_read(bi, mev)
                    res_ex_mil[j][ch] = mev
            wfree(sp1)
        down_done_mil[0] = B.mark_t()

    if stop == "ph6":
        B.wait("sync", res_ex_mil[KT - 1][CH - 1])
        B.dma("sync", dbg["RES"][:, :], RES[:, :], "sOUT")
        return finish()

    acc_mil = [None] * KT
    for j in range(KT):
        B.wait("gpsimd", res_ex_mil[j][0])
        B.wait("gpsimd", res_ex_mil[j][1])
        B.wait("gpsimd", spill_mil[j])
        acc_mil[j] = B.dma("gpsimd", RES[:, j * TT:(j + 1) * TT],
                           x1sp[:, j * TT:(j + 1) * TT], "sAC", accum_op=ALU.add)
    sq2_free = [None, None]
    last_sx = [None, None]
    last_sxx = [None, None]
    for sbi in (4, 5, 6, 7):
        bank_open(sbi)
    for j in range(KT):
        for ch in range(CH):
            rsl = RES[:, j * TT + ch * CW: j * TT + (ch + 1) * CW]
            B.wait("scalar", acc_mil[j])
            if sq2_free[ch] is not None:
                B.wait("scalar", sq2_free[ch])
            msq = B.a(lambda e, o=sg[ch][:, :], i=rsl: e.activation(o, i, AF.Square))
            B.wait("tensor", acc_mil[j])
            mS = B.t(lambda e, o=pm[4 + ch][0:1, :], l=onesc[:, :], r=rsl,
                     st=(j == 0), sp=(j == KT - 1):
                     e.matmul(o, l, r, start=st, stop=sp))
            last_sx[ch] = mS
            B.wait("tensor", msq)
            mQ = B.t(lambda e, o=pm[6 + ch][0:1, :], l=onesc[:, :], r=sg[ch][:, :],
                     st=(j == 0), sp=(j == KT - 1):
                     e.matmul(o, l, r, start=st, stop=sp))
            last_sxx[ch] = mQ
            sq2_free[ch] = mQ
    for ch in range(CH):
        pm_w[4 + ch] = last_sx[ch]
        pm_w[6 + ch] = last_sxx[ch]
    nm_bi, rs_bi = ln_rows_and_bcast()
    for j in range(KT):
        m3 = None
        for ch in range(CH):
            rsl = RES[:, j * TT + ch * CW: j * TT + (ch + 1) * CW]
            bank_read(nm_bi[ch], "vector")
            bank_read(rs_bi[ch], "vector")
            m1 = B.v(lambda e, o=rsl, a_=rsl, b_=pm[nm_bi[ch]][:, :]:
                     e.tensor_add(o, a_, b_))
            m3 = B.v(lambda e, o=rsl, a_=rsl, b_=pm[rs_bi[ch]][:, :]:
                     e.tensor_mul(o, a_, b_))
            bank_mark_read(nm_bi[ch], m1)
            bank_mark_read(rs_bi[ch], m3)
        B.wait("sync", m3)
        B.dma("sync", OUT[j * 128:(j + 1) * 128, :], RES[:, j * TT:(j + 1) * TT], "sOUT")

    return finish()



def make_inmaps(inputs):
    x = np.asarray(inputs["L0_attn_x"], dtype=np.float32)
    f32 = lambda a: np.ascontiguousarray(np.asarray(a, dtype=np.float32))
    WIN = f32(inputs["L0_attn_W_in"])
    WDEC = f32(inputs["L0_attn_W_decay"])
    WGATE = f32(inputs["L0_attn_W_gate"])
    WOUT = f32(inputs["L0_attn_W_out"])
    bsh = lambda b: np.ascontiguousarray(np.asarray(b, np.float32).reshape(KT, 128).T)
    BIN = bsh(inputs["L0_attn_b_in"])
    BDEC = bsh(inputs["L0_attn_b_decay"])
    DBIAS = bsh(inputs["L0_attn_decay_bias"])
    BGATE = bsh(inputs["L0_attn_b_gate"])
    BOUT = bsh(inputs["L0_attn_b_out"])
    wr = np.asarray(inputs["L0_ffn_w_router"], np.float32)
    WRH = np.ascontiguousarray(wr.T.reshape(KT, 128, E).transpose(1, 0, 2).reshape(128, KT * E))
    gw = np.asarray(inputs["L0_ffn_gate_w"], np.float32)
    GWH = np.ascontiguousarray(gw.reshape(KT, 128).T)
    SWG = f32(np.asarray(inputs["L0_ffn_sw_gate"]).T)
    SWU = f32(np.asarray(inputs["L0_ffn_sw_up"]).T)
    SWD = f32(np.asarray(inputs["L0_ffn_sw_down"]).T)
    WGT = f32(np.asarray(inputs["L0_ffn_wg"]).transpose(0, 2, 1))
    WUT = f32(np.asarray(inputs["L0_ffn_wu"]).transpose(0, 2, 1))
    WDT = f32(np.asarray(inputs["L0_ffn_wd"]).transpose(0, 2, 1))
    IDENT = np.eye(128, dtype=np.float32)
    CSTS = np.zeros((128, 8), np.float32)
    CSTS[:, 0] = EPS
    CSTS[:, 1] = -0.5
    CSTS[:, 2] = -1.0
    ONESR = np.ones((1, 128), np.float32)
    ONESC = np.ones((128, 1), np.float32)
    BC8 = np.zeros((E, E * 128), np.float32)
    for e in range(E):
        BC8[e, e * 128:(e + 1) * 128] = 1.0
    maps = []
    for c in range(NCORE):
        b, q = c // 4, c % 4
        XTC = np.ascontiguousarray(x[b, q * TT:(q + 1) * TT, :].T)
        SELMC = np.zeros((128, 128), np.float32)
        SELMC[:, c * 16:(c + 1) * 16] = 1.0
        maps.append(dict(
            XT=XTC, WIN=WIN, WDEC=WDEC, WGATE=WGATE, WOUT=WOUT,
            BIN=BIN, BDEC=BDEC, DBIAS=DBIAS, BGATE=BGATE, BOUT=BOUT,
            WRH=WRH, GWH=GWH, SWG=SWG, SWU=SWU, SWD=SWD,
            WGT=WGT, WUT=WUT, WDT=WDT,
            IDENT=IDENT, ONESR=ONESR, ONESC=ONESC, BC8=BC8, SELM=SELMC, CSTS=CSTS,
        ))
    return maps


def assemble(results):
    outs = []
    for b in range(2):
        cols = np.concatenate([results[4 * b + q]["OUT"] for q in range(4)], axis=1)
        outs.append(cols.T)
    return np.ascontiguousarray(np.stack(outs, axis=0).astype(np.float32))


_CACHED = {}


def kernel(**inputs):
    from concourse.bass_utils import run_bass_kernel_spmd
    maps = make_inmaps(inputs)
    nc = build_nc(stop="full", debug=False)
    res = run_bass_kernel_spmd(nc, maps, core_ids=list(range(NCORE)))
    return assemble(res.results)


# version 03
# speedup vs baseline: 1.0383x; 1.0383x over previous
import os
import numpy as np
import concourse.bass as bass
import concourse.mybir as mybir

F32 = mybir.dt.float32
BF16 = mybir.dt.bfloat16
AF = mybir.ActivationFunctionType
ALU = mybir.AluOpType
AX = mybir.AxisListType

D = 2048
TT = 1024
KT = 16
CH = 2
CW = 512
IS = 4096
II = 1024
E = 8
NCORE = 8
EPS = 1e-5
NSLOT = 3

PHASES = ["ph1", "ph25", "ph3", "ph4", "ph5a", "ph5b", "ph5", "ph6", "full"]


class Sem:
    def __init__(self, h):
        self.h = h
        self.n = 0


class Mil:
    def __init__(self, eng, sem, cnt, async_=False):
        self.eng, self.sem, self.cnt, self.async_ = eng, sem, cnt, async_


class Builder:
    def __init__(self, nc):
        self.nc = nc
        self.q = {k: [] for k in ("tensor", "vector", "scalar", "gpsimd", "sync")}
        self.sems = {}
        self._semctx = []
        self.last_sp = {}

    def sem(self, name):
        if name not in self.sems:
            ctx = self.nc.semaphore(name)
            self.sems[name] = Sem(ctx.__enter__())
            self._semctx.append(ctx)
        return self.sems[name]

    def emit(self, eng, fn):
        self.q[eng].append(fn)

    def wait(self, eng, mil):
        if mil is None:
            return
        if mil.eng == eng and not mil.async_:
            return
        h, c = mil.sem.h, mil.cnt
        self.emit(eng, lambda e, h=h, c=c: e.wait_ge(h, c))

    def mark_t(self):
        s = self.sem("sPE")
        prev = s.n
        if prev >= 1:
            self.emit("tensor", lambda e, h=s.h, c=prev: e.wait_ge(h, c))
        s.n += 1
        cnt = s.n
        self.emit("tensor", lambda e, h=s.h: e.nop().then_inc(h, 1))
        return Mil("tensor", s, cnt)

    def op(self, eng, fn, sp=False):
        s = self.sem({"tensor": "sPE", "vector": "sDVE", "scalar": "sACT",
                      "gpsimd": "sGP"}[eng])
        if eng in ("vector", "scalar") and s.n >= 1:
            prev = s.n
            self.emit(eng, lambda e, h=s.h, c=prev: e.wait_ge(h, c))
        self.last_sp[eng] = sp
        s.n += 1
        cnt = s.n
        def run(e, fn=fn, h=s.h):
            fn(e).then_inc(h, 1)
        self.emit(eng, run)
        return Mil(eng, s, cnt)

    def v(self, fn, sp=False):
        return self.op("vector", fn, sp=sp)

    def a(self, fn, sp=False):
        return self.op("scalar", fn, sp=sp)

    def t(self, fn):
        return self.op("tensor", fn)

    def t_plain(self, fn):
        self.emit("tensor", lambda e, fn=fn: fn(e))

    def dma(self, queue, out_ap, in_ap, semname, **kw):
        s = self.sem(semname)
        s.n += 16
        cnt = s.n
        def run(e, o=out_ap, i=in_ap, h=s.h, kw=kw):
            e.dma_start(out=o, in_=i, **kw).then_inc(h, 16)
        self.emit(queue, run)
        return Mil(queue, s, cnt, async_=True)


def build_nc(stop="full", debug=False):
    nc = bass.Bass("TRN2", target_bir_lowering=False)
    B = Builder(nc)

    def din(name, shape):
        return nc.dram_tensor(name, shape, F32, kind="ExternalInput").ap()

    XT = din("XT", [D, TT])
    WIN = din("WIN", [D, D])
    WDEC = din("WDEC", [D, D])
    WGATE = din("WGATE", [D, D])
    WOUT = din("WOUT", [D, D])
    BIN = din("BIN", [128, KT])
    BDEC = din("BDEC", [128, KT])
    DBIAS = din("DBIAS", [128, KT])
    BGATE = din("BGATE", [128, KT])
    BOUT = din("BOUT", [128, KT])
    WRH = din("WRH", [128, KT * E])
    GWH = din("GWH", [128, KT])
    SWG = din("SWG", [D, IS])
    SWU = din("SWU", [D, IS])
    SWD = din("SWD", [IS, D])
    WGT = din("WGT", [E, D, II])
    WUT = din("WUT", [E, D, II])
    WDT = din("WDT", [E, II, D])
    IDENT = din("IDENT", [128, 128])
    ONESR = din("ONESR", [1, 128])
    ONESC = din("ONESC", [128, 1])
    BC8 = din("BC8", [E, E * 128])
    SELM = din("SELM", [128, 128])
    CSTS = din("CSTS", [128, 8])

    OUT = nc.dram_tensor("OUT", [D, TT], F32, kind="ExternalOutput").ap()

    cc_in = nc.dram_tensor("cc_in", [128, 32], F32, kind="Internal")
    cc_out = nc.dram_tensor("cc_out", [128 * NCORE, 32], F32, kind="Internal",
                            addr_space="Local")
    x1sp = nc.dram_tensor("x1sp", [128, KT * TT], F32, kind="Internal").ap()

    dbg = {}
    def dout(name, shape):
        return nc.dram_tensor(name, shape, F32, kind="ExternalOutput").ap()
    if debug:
        if stop == "ph1":
            dbg["SL"] = dout("DBG_SL", [128, KT * TT])
            dbg["PC"] = dout("DBG_PC", [128, KT * TT])
            dbg["CONTRIB"] = dout("DBG_CONTRIB", [128, 32])
        if stop == "ph25":
            dbg["Y"] = dout("DBG_Y", [128, KT * TT])
            dbg["CARRY"] = dout("DBG_CARRY", [128, KT])
            dbg["GAT"] = dout("DBG_GAT", [128, NCORE * 32])
        if stop == "ph3":
            dbg["X1"] = dout("DBG_X1", [128, KT * TT])
        if stop == "ph4":
            dbg["COMB"] = dout("DBG_COMB", [128, 64])
            dbg["GROW"] = dout("DBG_GROW", [1, TT])
            dbg["CBT"] = dout("DBG_CBT", [E, TT])
        if stop in ("ph5", "ph6"):
            dbg["RES"] = dout("DBG_RES", [128, KT * TT])
        if stop == "ph5a":
            dbg["HB"] = dout("DBG_HB", [128, 16 * TT])
        if stop == "ph5b":
            dbg["RES"] = dout("DBG_RES", [128, KT * TT])

    base = (nc.sbuf_base + 31) & ~31
    off = [base]

    def at(name, shape, dtype):
        h = nc.alloc_sbuf_tensor_at(name, shape, dtype, offset=off[0])
        sz = int(np.prod(shape[1:])) * mybir.dt.size(dtype)
        off[0] = (off[0] + sz + 31) & ~31
        return h

    ident = at("ident", [128, 128], F32)
    onesr = at("onesr", [1, 128], F32)
    onesc = at("onesc", [128, 1], F32)
    bc8 = at("bc8", [E, E * 128], BF16)
    selm = at("selm", [128, 128], F32)
    csts = at("csts", [128, 8], F32)
    b_in = at("b_in", [128, KT], F32)
    b_dec = at("b_dec", [128, KT], F32)
    b_db = at("b_db", [128, KT], F32)
    b_gate = at("b_gate", [128, KT], F32)
    b_out = at("b_out", [128, KT], F32)
    wrb = at("wrb", [128, KT * E], BF16)
    gwb = at("gwb", [128, KT], BF16)
    carry = at("carry", [128, KT], F32)
    tcar = at("tcar", [128, KT], F32)
    gat = at("gat", [128, NCORE * 32], F32)
    call = at("call", [128, 128], F32)
    selt = at("selt", [128, 128], F32)
    contrib = at("contrib", [128, 32], F32)
    comb = at("comb", [128, 64], F32)
    combT = at("combT", [E, TT], BF16)
    rows_nm = at("rows_nm", [1, TT], F32)
    rows_rs = at("rows_rs", [1, TT], F32)
    rows_tmp = at("rows_tmp", [1, TT], F32)
    cbc = [at("cbc0", [128, TT], BF16), at("cbc1", [128, TT], BF16)]
    gbc = at("gbc", [128, TT], BF16)
    rssb = at("rssb", [128, TT], F32)
    nmsb = at("nmsb", [128, TT], F32)
    onescb = at("onescb", [128, 1], BF16)
    rmx = at("rmx", [128, 1], F32)
    rm2 = at("rm2", [128, 1], F32)
    msk = at("msk", [128, 8], F32)
    mx8t = at("mx8t", [128, 8], F32)
    e1t = at("e1t", [128, 8], F32)
    dent = at("dent", [128, 1], F32)
    wsl = [at(f"wsl{i}", [128, 4096], BF16) for i in range(NSLOT)]

    U = off[0]
    xb = nc.alloc_sbuf_tensor_at("xb", [128, KT * TT], BF16, offset=U)
    Sl = nc.alloc_sbuf_tensor_at("Sl", [128, KT * TT], F32, offset=U + 32768)
    Pc = nc.alloc_sbuf_tensor_at("Pc", [128, KT * TT], BF16, offset=U + 98304)
    yv = Pc
    scr0 = U + 131072
    decay_t = nc.alloc_sbuf_tensor_at("decay_t", [128, TT], F32, offset=scr0)
    v_t = nc.alloc_sbuf_tensor_at("v_t", [128, TT], F32, offset=scr0 + 4096)
    zero_t = nc.alloc_sbuf_tensor_at("zero_t", [128, TT], F32, offset=scr0 + 8192)
    sg = [nc.alloc_sbuf_tensor_at("sg0", [128, CW], F32, offset=scr0 + 12288),
          nc.alloc_sbuf_tensor_at("sg1", [128, CW], F32, offset=scr0 + 14336)]
    gate_t = [nc.alloc_sbuf_tensor_at("gt0", [128, TT], BF16, offset=scr0 + 16384),
              nc.alloc_sbuf_tensor_at("gt1", [128, TT], BF16, offset=scr0 + 18432),
              nc.alloc_sbuf_tensor_at("gt2", [128, TT], BF16, offset=scr0 + 12288),
              nc.alloc_sbuf_tensor_at("gt3", [128, TT], BF16, offset=scr0 + 14336)]
    RES = nc.alloc_sbuf_tensor_at("RES", [128, KT * TT], F32, offset=U)
    x1b = nc.alloc_sbuf_tensor_at("x1b", [128, KT * TT], BF16, offset=U + 65536)
    hbig = nc.alloc_sbuf_tensor_at("hbig", [128, 16 * TT], BF16, offset=U + 98304)
    end = scr0 + 20480
    assert end <= nc.sbuf_top, f"SBUF overflow: end={end} top={nc.sbuf_top}"

    pm = [nc.alloc_psum_tensor(f"pm{i}", [128, CW], F32) for i in range(8)]
    pm_w = [None] * 8
    pm_r = [[] for _ in range(8)]

    def bank_open(bi):
        for m in pm_r[bi]:
            B.wait("tensor", m)
        pm_r[bi] = []

    def bank_read(bi, eng):
        B.wait(eng, pm_w[bi])

    def bank_mark_read(bi, mil):
        pm_r[bi].append(mil)

    def mm_group(bi, pairs, out_ap, start=True, stop=True):
        n = len(pairs)
        for i, (l, r) in enumerate(pairs):
            st = start and (i == 0)
            sp = stop and (i == n - 1)
            if sp:
                pm_w[bi] = B.t(lambda e, o=out_ap, l=l, r=r, st=st:
                               e.matmul(o, l, r, start=st, stop=True))
            else:
                B.t_plain(lambda e, o=out_ap, l=l, r=r, st=st:
                          e.matmul(o, l, r, start=st, stop=False))

    wstate = {"next": 0, "uses": [0] * NSLOT}
    sWF = [B.sem(f"sWF{i}") for i in range(NSLOT)]
    sWL = [B.sem(f"sWL{i}") for i in range(NSLOT)]

    def wload(dram2d, kt_tiles, cols, split=2, dram3d=None):
        i = wstate["next"]
        wstate["next"] += 1
        s = i % NSLOT
        use = wstate["uses"][s]
        wstate["uses"][s] += 1
        if use >= 1:
            B.emit("gpsimd", lambda e, h=sWF[s].h, c=use: e.wait_ge(h, c))
        sb3 = wsl[s].rearrange("p (k c) -> p k c", k=kt_tiles)
        dr3 = dram3d if dram3d is not None else dram2d.rearrange("(k p) c -> p k c", p=128)
        if wslot_view0[0] is None and kt_tiles == 16:
            wslot_view0[0] = sb3
        kstep = kt_tiles // split
        for part in range(split):
            sWL[s].n += 16
            o = sb3[:, part * kstep:(part + 1) * kstep, :]
            i_ = dr3[:, part * kstep:(part + 1) * kstep, :]
            def run(e, o=o, i_=i_, h=sWL[s].h):
                e.dma_start(out=o, in_=i_).then_inc(h, 16)
            B.emit("gpsimd", run)
        return s, sWL[s].n, sb3

    wslot_view0 = [None]

    def wready(slot, cnt):
        B.emit("tensor", lambda e, h=sWL[slot].h, c=cnt: e.wait_ge(h, c))

    def wfree(slot):
        spe = B.sem("sPE")
        if spe.n >= 1:
            B.emit("tensor", lambda e, h=spe.h, c=spe.n: e.wait_ge(h, c))
        sWF[slot].n += 1
        B.emit("tensor", lambda e, h=sWF[slot].h: e.nop().then_inc(h, 1))

    for dst, src in ((ident, IDENT), (onesr, ONESR), (onesc, ONESC), (selm, SELM), (csts, CSTS),
                     (b_in, BIN), (b_dec, BDEC), (b_db, DBIAS), (b_gate, BGATE),
                     (b_out, BOUT)):
        consts_mil = B.dma("sync", dst[:, :], src[:, :], "sY")
    B.dma("gpsimd", bc8[:, :], BC8[:, :], "sX")
    B.wait("gpsimd", consts_mil)
    onescb_mil = B.dma("gpsimd", onescb[:, :], ONESC[:, :], "sX")
    B.dma("gpsimd", wrb[:, :], WRH[:, :], "sX")
    smalls_mil = B.dma("gpsimd", gwb[:, :], GWH[:, :], "sX")
    xb3 = xb.rearrange("p (k t) -> p k t", k=KT)
    xt3 = XT.rearrange("(k p) t -> p k t", p=128)
    B.dma("gpsimd", xb3[:, 0:8, :], xt3[:, 0:8, :], "sX")
    xb_mil = B.dma("gpsimd", xb3[:, 8:16, :], xt3[:, 8:16, :], "sX")

    B.wait("vector", consts_mil)
    bdec_mil = B.v(lambda e: e.tensor_add(b_dec[:, :], b_dec[:, :], b_db[:, :]))
    zero_mil = B.v(lambda e: e.memzero(zero_t[:, :]), sp=True)

    bank_rr = [0]
    def next_bank(nb=8, bbase=0):
        bi = bbase + bank_rr[0] % nb
        bank_rr[0] += 1
        return bi

    def proj_mm(vw, jj, ch, bi, rhs3=None):
        r3 = rhs3 if rhs3 is not None else xb3
        pairs = [(vw[:, k, jj * 128:(jj + 1) * 128],
                  r3[:, k, ch * CW:(ch + 1) * CW]) for k in range(KT)]
        mm_group(bi, pairs, pm[bi][:, :])

    def proj_mm_pair(vw, jj, biA, biB, rhs3=None, n_k=KT, koff=0,
                     start=True, stop=True):
        r3 = rhs3 if rhs3 is not None else xb3
        for k in range(n_k):
            l = vw[:, k, jj * 128:(jj + 1) * 128]
            r0 = r3[:, koff + k, 0:CW]
            r1 = r3[:, koff + k, CW:2 * CW]
            st = start and (k == 0)
            sp = stop and (k == n_k - 1)
            if sp:
                pm_w[biA] = B.t(lambda e, o=pm[biA][:, :], l=l, r=r0, st=st:
                                e.matmul(o, l, r, start=st, stop=True))
                pm_w[biB] = B.t(lambda e, o=pm[biB][:, :], l=l, r=r1, st=st:
                                e.matmul(o, l, r, start=st, stop=True))
            else:
                B.t_plain(lambda e, o=pm[biA][:, :], l=l, r=r0, st=st:
                          e.matmul(o, l, r, start=st, stop=False))
                B.t_plain(lambda e, o=pm[biB][:, :], l=l, r=r1, st=st:
                          e.matmul(o, l, r, start=st, stop=False))

    y_src_mil = [None] * KT
    scr_dec_free = None
    scr_sg_free = [None, None]
    first_x = [False]

    def wait_x_once():
        if not first_x[0]:
            B.wait("tensor", xb_mil)
            first_x[0] = True

    contrib_mil = None
    for pb in range(8):
        sd, cd, vd = wload(WDEC[:, pb * 256:(pb + 1) * 256], KT, 256)
        si, ci, vi = wload(WIN[:, pb * 256:(pb + 1) * 256], KT, 256)
        for jj in range(2):
            j = pb * 2 + jj
            dmils = []
            biA, biB = next_bank(), next_bank()
            bank_open(biA)
            bank_open(biB)
            if jj == 0:
                wready(sd, cd)
                wait_x_once()
            proj_mm_pair(vd, jj, biA, biB)
            for ch, bi in ((0, biA), (1, biB)):
                bank_read(bi, "scalar")
                if scr_dec_free is not None:
                    B.wait("scalar", scr_dec_free)
                if j == 0:
                    B.wait("scalar", bdec_mil)
                m = B.a(lambda e, o=decay_t[:, ch * CW:(ch + 1) * CW], i=pm[bi][:, :],
                        bi_=b_dec[:, j:j + 1]:
                        e.activation(o, i, AF.Sigmoid, bias=bi_))
                bank_mark_read(bi, m)
                dmils.append(m)
            if jj == 1:
                wfree(sd)
            biA, biB = next_bank(), next_bank()
            bank_open(biA)
            bank_open(biB)
            if jj == 0:
                wready(si, ci)
            proj_mm_pair(vi, jj, biA, biB)
            for ch, bi in ((0, biA), (1, biB)):
                bank_read(bi, "scalar")
                if scr_sg_free[ch] is not None:
                    B.wait("scalar", scr_sg_free[ch])
                m1 = B.a(lambda e, o=sg[ch][:, :], i=pm[bi][:, :], bi_=b_in[:, j:j + 1]:
                         e.activation(o, i, AF.Sigmoid, bias=bi_))
                bank_read(bi, "vector")
                B.wait("vector", m1)
                m2 = B.v(lambda e, o=v_t[:, ch * CW:(ch + 1) * CW], i=pm[bi][:, :],
                         bi_=b_in[:, j:j + 1], s_=sg[ch][:, :]:
                         e.scalar_tensor_tensor(o, i, bi_, s_, ALU.add, ALU.mult))
                bank_mark_read(bi, m2)
                B.wait("vector", dmils[ch])
                B.v(lambda e, o=sg[ch][:, :], d=decay_t[:, ch * CW:(ch + 1) * CW],
                    v=v_t[:, ch * CW:(ch + 1) * CW]:
                    e.tensor_mul(o, d, v))
                m4 = B.v(lambda e, o=v_t[:, ch * CW:(ch + 1) * CW],
                         a_=v_t[:, ch * CW:(ch + 1) * CW], b_=sg[ch][:, :]:
                         e.tensor_sub(o, a_, b_))
                scr_sg_free[ch] = m4
            if jj == 1:
                wfree(si)
            if j == 0:
                B.wait("vector", zero_mil)
            B.v(lambda e, o=Sl[:, j * TT:(j + 1) * TT], d=decay_t[:, :], v=v_t[:, :]:
                e.tensor_tensor_scan(o, d, v, 0.0, ALU.mult, ALU.add), sp=True)
            mpc = B.v(lambda e, o=Pc[:, j * TT:(j + 1) * TT], d=decay_t[:, :],
                      z=zero_t[:, :]:
                      e.tensor_tensor_scan(o, d, z, 1.0, ALU.mult, ALU.add), sp=True)
            B.v(lambda e, o=contrib[:, j:j + 1], i=Sl[:, (j + 1) * TT - 1:(j + 1) * TT]:
                e.tensor_copy(o, i))
            contrib_mil = B.v(
                lambda e, o=contrib[:, KT + j:KT + j + 1],
                i=Pc[:, (j + 1) * TT - 1:(j + 1) * TT]:
                e.tensor_copy(o, i))
            scr_dec_free = mpc

    def finish():
        sOUT = B.sems.get("sOUT")
        if sOUT is not None:
            B.emit("sync", lambda e, h=sOUT.h, c=sOUT.n: e.wait_ge(h, c))
        with nc.Block() as blk:
            blk.tensor(lambda e: [f(e) for f in B.q["tensor"]])
            blk.vector(lambda e: [f(e) for f in B.q["vector"]])
            blk.scalar(lambda e: [f(e) for f in B.q["scalar"]])
            blk.gpsimd(lambda e: [f(e) for f in B.q["gpsimd"]])
            blk.sync(lambda e: [f(e) for f in B.q["sync"]])
        return nc

    if stop == "ph1":
        B.wait("sync", contrib_mil)
        B.dma("sync", dbg["SL"][:, :], Sl[:, :], "sOUT")
        B.dma("sync", dbg["PC"][:, :], Pc.bitcast(BF16)[:, :], "sOUT") if False else None
        mcp = B.v(lambda e: e.tensor_copy(v_t[:, :], Pc[:, 0:TT]))
        B.wait("sync", mcp)
        B.dma("sync", dbg["PC"][:, 0:TT], v_t[:, :], "sOUT")
        B.dma("sync", dbg["CONTRIB"][:, :], contrib[:, :], "sOUT")
        return finish()

    wg_pieces = []
    for pb in range(6):
        wg_pieces.append(wload(WGATE[:, pb * 256:(pb + 1) * 256], KT, 256))
    B.wait("sync", contrib_mil)
    m_ccin = B.dma("sync", cc_in.ap()[:, :], contrib[:, :], "sY")
    sCC = B.sem("sCC")
    B.wait("gpsimd", m_ccin)
    sCC.n += 1
    def run_cc(e, h=sCC.h):
        e.collective_compute(
            "AllGather", ALU.bypass,
            replica_groups=[list(range(NCORE))],
            ins=[cc_in.ap().opt()],
            outs=[cc_out.ap().opt()],
        ).then_inc(h, 1)
    B.emit("gpsimd", run_cc)
    cc_mil = Mil("gpsimd", sCC, sCC.n, async_=True)
    B.wait("gpsimd", cc_mil)
    for r in range(NCORE):
        gat_m = B.dma("gpsimd", gat[:, r * 32:(r + 1) * 32],
                      cc_out.ap()[r * 128:(r + 1) * 128, :], "sX")
    B.wait("vector", gat_m)
    B.v(lambda e: e.memzero(call[:, :]), sp=True)
    for b in range(2):
        c0 = 4 * b
        def gF(c):
            return gat[:, c * 32:c * 32 + KT]
        def gP(c):
            return gat[:, c * 32 + KT:c * 32 + 2 * KT]
        def cnd(qq):
            i = c0 + qq
            return call[:, i * KT:(i + 1) * KT]
        B.v(lambda e, o=cnd(1), i=gF(c0 + 0): e.tensor_copy(o, i))
        B.v(lambda e, o=tcar[:, :], a_=gP(c0 + 1), b_=cnd(1): e.tensor_mul(o, a_, b_))
        B.v(lambda e, o=cnd(2), a_=gF(c0 + 1), b_=tcar[:, :]: e.tensor_add(o, a_, b_))
        B.v(lambda e, o=tcar[:, :], a_=gP(c0 + 2), b_=cnd(2): e.tensor_mul(o, a_, b_))
        B.v(lambda e, o=cnd(3), a_=gF(c0 + 2), b_=tcar[:, :]: e.tensor_add(o, a_, b_))
    B.v(lambda e: e.tensor_mul(selt[:, :], call[:, :], selm[:, :]))
    B.v(lambda e: e.tensor_add(selt[:, 0:64], selt[:, 0:64], selt[:, 64:128]))
    B.v(lambda e: e.tensor_add(selt[:, 0:32], selt[:, 0:32], selt[:, 32:64]))
    carry_mil = B.v(lambda e: e.tensor_add(carry[:, :], selt[:, 0:KT], selt[:, KT:2 * KT]))
    for pb in range(6, 8):
        wg_pieces.append(wload(WGATE[:, pb * 256:(pb + 1) * 256], KT, 256))

    y_mil = [None] * KT
    gate_free = [None, None, scr_sg_free[0], scr_sg_free[1]]
    for pb in range(8):
        sgt, cg, vg = wg_pieces[pb]
        for jj in range(2):
            j = pb * 2 + jj
            gmils = []
            biA, biB = next_bank(), next_bank()
            bank_open(biA)
            bank_open(biB)
            if jj == 0:
                wready(sgt, cg)
            proj_mm_pair(vg, jj, biA, biB)
            for ch, bi in ((0, biA), (1, biB)):
                bank_read(bi, "scalar")
                if gate_free[j % 4] is not None:
                    B.wait("scalar", gate_free[j % 4])
                m = B.a(lambda e, o=gate_t[j % 4][:, ch * CW:(ch + 1) * CW],
                        i=pm[bi][:, :], bi_=b_gate[:, j:j + 1]:
                        e.activation(o, i, AF.Sigmoid, bias=bi_))
                bank_mark_read(bi, m)
                gmils.append(m)
            if jj == 1:
                wfree(sgt)
            B.wait("vector", gmils[1])
            B.v(lambda e, o=v_t[:, :], i=Pc[:, j * TT:(j + 1) * TT], s=carry[:, j:j + 1]:
                e.tensor_scalar(o, i, s, None, ALU.mult))
            B.v(lambda e, o=v_t[:, :], a_=v_t[:, :], b_=Sl[:, j * TT:(j + 1) * TT]:
                e.tensor_add(o, a_, b_))
            my = B.v(lambda e, o=yv[:, j * TT:(j + 1) * TT], a_=v_t[:, :],
                     b_=gate_t[j % 4][:, :]:
                     e.tensor_mul(o, a_, b_))
            y_mil[j] = my
            gate_free[j % 4] = my

    if stop == "ph25":
        mcp = B.v(lambda e: e.tensor_copy(Sl[:, :], yv[:, :]))
        B.wait("sync", mcp)
        B.dma("sync", dbg["Y"][:, :], Sl[:, :], "sOUT")
        B.dma("sync", dbg["CARRY"][:, :], carry[:, :], "sOUT")
        B.dma("sync", dbg["GAT"][:, :], gat[:, :], "sOUT")
        return finish()

    B.wait("sync", y_mil[KT - 1])
    res3 = RES.rearrange("p (k t) -> p k t", k=KT)
    m_pre0 = B.dma("sync", res3[:, 0:8, :], xt3[:, 0:8, :], "sY")
    m_pre1 = B.dma("sync", res3[:, 8:16, :], xt3[:, 8:16, :], "sY")
    yb3 = yv.rearrange("p (k t) -> p k t", k=KT)
    xcast = gate_t[0]
    sqcast = gate_t[1]
    cast_free = [None, None]
    sq_free = [None, None]
    y_waited = [False]
    stats_opened = [False]
    last_sx = [None, None]
    last_sxx = [None, None]
    for pb in range(8):
        so, co, vo = wload(WOUT[:, pb * 256:(pb + 1) * 256], KT, 256)
        for jj in range(2):
            j = pb * 2 + jj
            biA, biB = next_bank(4), next_bank(4)
            bank_open(biA)
            bank_open(biB)
            if jj == 0:
                wready(so, co)
            if not y_waited[0]:
                B.wait("tensor", y_mil[KT - 1])
                B.wait("tensor", consts_mil)
                B.wait("tensor", onescb_mil)
                y_waited[0] = True
            proj_mm_pair(vo, jj, biA, biB, rhs3=yb3)
            for ch, bi in ((0, biA), (1, biB)):
                bank_read(bi, "vector")
                if j < 8:
                    B.wait("vector", m_pre0)
                elif j == 8 and ch == 0:
                    B.wait("vector", m_pre1)
                rsl = RES[:, j * TT + ch * CW: j * TT + (ch + 1) * CW]
                mev = B.v(lambda e, o=rsl, i=pm[bi][:, :], b_=b_out[:, j:j + 1], r=rsl:
                          e.scalar_tensor_tensor(o, i, b_, r, ALU.add, ALU.add))
                bank_mark_read(bi, mev)
                if cast_free[ch] is not None:
                    B.wait("vector", cast_free[ch])
                mxc = B.v(lambda e, o=xcast[:, ch * CW:(ch + 1) * CW], i=rsl:
                          e.tensor_copy(o, i))
                B.wait("scalar", mev)
                if sq_free[ch] is not None:
                    B.wait("scalar", sq_free[ch])
                msq = B.a(lambda e, o=sqcast[:, ch * CW:(ch + 1) * CW], i=rsl:
                          e.activation(o, i, AF.Square))
                if not stats_opened[0]:
                    for sbi in (4, 5, 6, 7):
                        bank_open(sbi)
                    stats_opened[0] = True
                B.wait("tensor", mxc)
                mS = B.t(lambda e, o=pm[4 + ch][0:1, :], l=onescb[:, :],
                         r=xcast[:, ch * CW:(ch + 1) * CW],
                         st=(j == 0), sp=(j == KT - 1):
                         e.matmul(o, l, r, start=st, stop=sp))
                last_sx[ch] = mS
                cast_free[ch] = mS
                B.wait("tensor", msq)
                mQ = B.t(lambda e, o=pm[6 + ch][0:1, :], l=onescb[:, :],
                         r=sqcast[:, ch * CW:(ch + 1) * CW],
                         st=(j == 0), sp=(j == KT - 1):
                         e.matmul(o, l, r, start=st, stop=sp))
                last_sxx[ch] = mQ
                sq_free[ch] = mQ
        wfree(so)
    for ch in range(CH):
        pm_w[4 + ch] = last_sx[ch]
        pm_w[6 + ch] = last_sxx[ch]

    rows_tensor_read = [None]

    def ln_rows_and_bcast():
        mvar = None
        for ch in range(CH):
            bank_read(4 + ch, "vector")
            bank_read(6 + ch, "vector")
            if rows_tensor_read[0] is not None:
                B.wait("vector", rows_tensor_read[0])
                rows_tensor_read[0] = None
            mnm = B.v(lambda e, o=rows_nm[0:1, ch * CW:(ch + 1) * CW],
                      i=pm[4 + ch][0:1, :]:
                      e.tensor_scalar(o, i, -1.0 / D, None, ALU.mult))
            B.v(lambda e, o=rows_tmp[0:1, ch * CW:(ch + 1) * CW],
                a_=rows_nm[0:1, ch * CW:(ch + 1) * CW],
                b_=rows_nm[0:1, ch * CW:(ch + 1) * CW]:
                e.tensor_mul(o, a_, b_))
            B.v(lambda e, o=rows_tmp[0:1, ch * CW:(ch + 1) * CW],
                i=rows_tmp[0:1, ch * CW:(ch + 1) * CW]:
                e.tensor_scalar(o, i, -EPS, None, ALU.add))
            mvar = B.v(lambda e, o=rows_rs[0:1, ch * CW:(ch + 1) * CW],
                       i=pm[6 + ch][0:1, :],
                       t_=rows_tmp[0:1, ch * CW:(ch + 1) * CW]:
                       e.scalar_tensor_tensor(o, i, 1.0 / D, t_, ALU.mult, ALU.subtract))
            bank_mark_read(4 + ch, mnm)
            bank_mark_read(6 + ch, mvar)
        B.wait("scalar", mvar)
        B.a(lambda e: e.activation(rows_tmp[0:1, :], rows_rs[0:1, :], AF.Ln), sp=True)
        mrs = B.a(lambda e: e.activation(rows_rs[0:1, :], rows_tmp[0:1, :], AF.Exp,
                                         scale=csts[0:1, 1:2]), sp=True)
        B.wait("tensor", mrs)
        nm_bi, rs_bi = [0, 1], [2, 3]
        lastb = None
        for ch in range(CH):
            for bi, row in ((nm_bi[ch], rows_nm), (rs_bi[ch], rows_rs)):
                bank_open(bi)
                pm_w[bi] = B.t(lambda e, o=pm[bi][:, :],
                               r=row[0:1, ch * CW:(ch + 1) * CW]:
                               e.matmul(o, onesr[0:1, :], r, start=True, stop=True))
                lastb = pm_w[bi]
        rows_tensor_read[0] = lastb
        return nm_bi, rs_bi

    nm_bi, rs_bi = ln_rows_and_bcast()
    for ch in range(CH):
        bank_read(rs_bi[ch], "vector")
        mrscp = B.v(lambda e, o=rssb[:, ch * CW:(ch + 1) * CW], i=pm[rs_bi[ch]][:, :]:
                    e.tensor_copy(o, i))
        bank_mark_read(rs_bi[ch], mrscp)
        bank_read(nm_bi[ch], "vector")
        mnmcp = B.v(lambda e, o=nmsb[:, ch * CW:(ch + 1) * CW], i=pm[nm_bi[ch]][:, :]:
                    e.tensor_copy(o, i))
        bank_mark_read(nm_bi[ch], mnmcp)
    x1_mil = [None] * KT
    for j in range(KT):
        B.v(lambda e, o=RES[:, j * TT:(j + 1) * TT],
            a_=RES[:, j * TT:(j + 1) * TT], b_=nmsb[:, :]:
            e.tensor_add(o, a_, b_))
        m2 = B.v(lambda e, o=x1b[:, j * TT:(j + 1) * TT],
                 a_=RES[:, j * TT:(j + 1) * TT], b_=rssb[:, :]:
                 e.tensor_mul(o, a_, b_))
        x1_mil[j] = m2
    ln1_fixup = []
    for j in range(KT):
        def _fix(j=j):
            m3 = B.v(lambda e, o=RES[:, j * TT:(j + 1) * TT],
                     a_=RES[:, j * TT:(j + 1) * TT], b_=rssb[:, :]:
                     e.tensor_mul(o, a_, b_))
            B.wait("sync", m3)
            return B.dma("sync", x1sp[:, j * TT:(j + 1) * TT],
                         RES[:, j * TT:(j + 1) * TT], "sY")
        ln1_fixup.append(_fix)

    if os.environ.get("FIXUP_EARLY"):
        spill_mil = [None] * KT
        for j in range(KT):
            spill_mil[j] = ln1_fixup[j]()
    if stop == "ph3":
        if not os.environ.get("FIXUP_EARLY"):
            spill_mil = [None] * KT
            for j in range(KT):
                spill_mil[j] = ln1_fixup[j]()
        B.dma("sync", dbg["X1"][:, :], RES[:, :], "sOUT")
        return finish()

    x1b3 = x1b.rearrange("p (k t) -> p k t", k=KT)
    comb_mil = [None] * 8
    first_r = [False]
    for tg in range(8):
        bi = next_bank(4)
        bank_open(bi)
        if not first_r[0]:
            B.wait("tensor", smalls_mil)
            B.wait("tensor", x1_mil[KT - 1])
            first_r[0] = True
        pairs = [(x1b3[:, k, tg * 128:(tg + 1) * 128], wrb[:, k * 8:(k + 1) * 8])
                 for k in range(KT)]
        mm_group(bi, pairs, pm[bi][:, 0:8])
        p8 = pm[bi][:, 0:8]
        bank_read(bi, "vector")
        mlg = B.v(lambda e, o=msk[:, :], i=p8: e.tensor_copy(o, i))
        bank_mark_read(bi, mlg)
        B.v(lambda e, o=mx8t[:, :], i=msk[:, :]: e.max(out=o, in_=i), sp=True)
        B.v(lambda e, o=mx8t[:, 2:8]: e.memset(o, -1e30), sp=True)
        B.v(lambda e, o=e1t[:, :], m=mx8t[:, :], i=msk[:, :]:
            e.match_replace(out=o, in_to_replace=m, in_values=i, imm_value=-1e30), sp=True)
        mneg = B.v(lambda e, o=rmx[:, :], i=mx8t[:, 0:1]: e.tensor_scalar_mul(o, i, -1.0))
        B.wait("scalar", mneg)
        B.a(lambda e, o=msk[:, :], i=msk[:, :]:
            e.activation(o, i, AF.Exp, bias=rmx[:, 0:1]), sp=True)
        mez = B.a(lambda e, o=e1t[:, :], i=e1t[:, :]:
                  e.activation(o, i, AF.Exp, bias=rmx[:, 0:1]), sp=True)
        B.wait("vector", mez)
        B.v(lambda e, o=e1t[:, :], a_=msk[:, :], b_=e1t[:, :]: e.tensor_sub(o, a_, b_))
        B.v(lambda e, o=dent[:, :], i=e1t[:, :]: e.reduce_sum(o, i, axis=AX.X), sp=True)
        B.v(lambda e, o=rm2[:, :], i=dent[:, :]: e.reciprocal(o, i), sp=True)
        mc = B.v(lambda e, o=comb[:, tg * 8:(tg + 1) * 8], i=e1t[:, :]:
                 e.tensor_scalar(o, i, rm2[:, 0:1], None, ALU.mult))
        comb_mil[tg] = mc
    combT_mil = None
    for half in range(2):
        for tgi in range(4):
            tg = half * 4 + tgi
            bi = 4 + tgi
            bank_open(bi)
            B.wait("tensor", comb_mil[tg])
            pm_w[bi] = B.t(lambda e, o=pm[bi][0:8, 0:128],
                           i=comb[:, tg * 8:(tg + 1) * 8]:
                           e.transpose(o, i, ident[:, :]))
            bank_read(bi, "vector")
            mct = B.v(lambda e, o=combT[0:8, tg * 128:(tg + 1) * 128],
                      i=pm[bi][0:8, 0:128]:
                      e.tensor_copy(o, i))
            bank_mark_read(bi, mct)
            combT_mil = mct
    mexr = None
    for ch in range(CH):
        bi = 6 + ch
        bank_open(bi)
        pairs = [(gwb[:, k:k + 1], x1b3[:, k, ch * CW:(ch + 1) * CW]) for k in range(KT)]
        mm_group(bi, pairs, pm[bi][0:1, :])
        bank_read(bi, "scalar")
        mexr = B.a(lambda e, o=rows_tmp[0:1, ch * CW:(ch + 1) * CW], i=pm[bi][0:1, :]:
                   e.activation(o, i, AF.Exp, scale=csts[0:1, 2:3]))
        bank_mark_read(bi, mexr)
    B.wait("vector", mexr)
    if rows_tensor_read[0] is not None:
        B.wait("vector", rows_tensor_read[0])
        rows_tensor_read[0] = None
    B.v(lambda e: e.tensor_scalar(rows_nm[0:1, :], rows_tmp[0:1, :], 1.0, None, ALU.add))
    grow_mil = B.v(lambda e: e.reciprocal(rows_rs[0:1, :], rows_nm[0:1, :]), sp=True)
    B.wait("tensor", grow_mil)
    gbc_mil = None
    for ch in range(CH):
        bi = 4 + ch
        bank_open(bi)
        pm_w[bi] = B.t(lambda e, o=pm[bi][:, :], r=rows_rs[0:1, ch * CW:(ch + 1) * CW]:
                       e.matmul(o, onesr[0:1, :], r, start=True, stop=True))
        rows_tensor_read[0] = pm_w[bi]
        bank_read(bi, "vector")
        mg = B.v(lambda e, o=gbc[:, ch * CW:(ch + 1) * CW], i=pm[bi][:, :]:
                 e.tensor_copy(o, i))
        bank_mark_read(bi, mg)
        gbc_mil = mg

    if stop == "ph4":
        B.wait("sync", gbc_mil)
        B.dma("sync", dbg["COMB"][:, :], comb[:, :], "sOUT")
        B.dma("sync", dbg["GROW"][0:1, :], rows_rs[0:1, :], "sOUT")
        B.wait("gpsimd", combT_mil)
        B.dma("gpsimd", dbg["CBT"][:, :], combT[:, :], "sOUT")
        return finish()

    hb3 = hbig.rearrange("p (k t) -> p k t", k=16)
    uscr = gate_t
    uscr_free = [None, None]
    if not os.environ.get("FIXUP_EARLY"):
        spill_mil = [None] * KT
        for j in range(KT):
            spill_mil[j] = ln1_fixup[j]()
    spill_all = spill_mil[KT - 1]
    res_sh_mil = [[None] * CH for _ in range(KT)]
    down_done_mil = [None]
    for H in range(2):
        h_last = None
        for pb in range(8):
            sgp, cgp, vgp = wload(SWG[:, H * 2048 + pb * 256: H * 2048 + (pb + 1) * 256],
                                  KT, 256)
            sup, cup, vup = wload(SWU[:, H * 2048 + pb * 256: H * 2048 + (pb + 1) * 256],
                                  KT, 256)
            for jj in range(2):
                it = pb * 2 + jj
                umils = []
                gpair = [(0, next_bank()), (1, next_bank())]
                for ch, bi in gpair:
                    bank_open(bi)
                    if jj == 0 and ch == 0:
                        wready(sgp, cgp)
                    proj_mm(vgp, jj, ch, bi, rhs3=x1b3)
                    bank_read(bi, "scalar")
                    if down_done_mil[0] is not None:
                        B.wait("scalar", down_done_mil[0])
                    mgs = B.a(lambda e, o=hb3[:, it, ch * CW:(ch + 1) * CW],
                              i=pm[bi][:, :]:
                              e.activation(o, i, AF.Silu))
                    bank_mark_read(bi, mgs)
                    umils.append(mgs)
                if jj == 1:
                    wfree(sgp)
                upair = [(0, next_bank()), (1, next_bank())]
                for ch, bi in upair:
                    bank_open(bi)
                    if jj == 0 and ch == 0:
                        wready(sup, cup)
                    proj_mm(vup, jj, ch, bi, rhs3=x1b3)
                    bank_read(bi, "scalar")
                    if uscr_free[it % 2] is not None:
                        B.wait("scalar", uscr_free[it % 2])
                    ucp = B.a(lambda e, o=uscr[it % 2][:, ch * CW:(ch + 1) * CW],
                              i=pm[bi][:, :]:
                              e.copy(o, i))
                    bank_mark_read(bi, ucp)
                    B.wait("vector", umils[ch])
                    B.wait("vector", ucp)
                    if down_done_mil[0] is not None:
                        B.wait("vector", down_done_mil[0])
                    mh = B.v(lambda e, o=hb3[:, it, ch * CW:(ch + 1) * CW],
                             a_=hb3[:, it, ch * CW:(ch + 1) * CW],
                             b_=uscr[it % 2][:, ch * CW:(ch + 1) * CW]:
                             e.tensor_mul(o, a_, b_))
                    h_last = mh
                    uscr_free[it % 2] = mh
                if jj == 1:
                    wfree(sup)
        if stop == "ph5a" and H == 0:
            B.wait("gpsimd", h_last)
            B.dma("gpsimd", dbg["HB"][:, :], hbig[:, :], "sOUT")
            return finish()
        down_done_mil[0] = None
        h_all = h_last
        SWD3 = SWD.rearrange("(k p) c -> p k c", p=128)
        for pb in range(8):
            sdp, cdp, vdp = wload(None, KT, 256,
                                  dram3d=SWD3[:, H * 16:(H + 1) * 16,
                                              pb * 256:(pb + 1) * 256])
            for jj in range(2):
                j = pb * 2 + jj
                biA, biB = next_bank(), next_bank()
                bank_open(biA)
                bank_open(biB)
                if jj == 0:
                    wready(sdp, cdp)
                    if pb == 0:
                        B.wait("tensor", h_all)
                for ch, bi in ((0, biA), (1, biB)):
                    pairs = [(vdp[:, k, jj * 128:(jj + 1) * 128],
                              hb3[:, k, ch * CW:(ch + 1) * CW]) for k in range(KT)]
                    mm_group(bi, pairs, pm[bi][:, :])
                for ch, bi in ((0, biA), (1, biB)):
                    rsl = RES[:, j * TT + ch * CW: j * TT + (ch + 1) * CW]
                    if H == 0:
                        bank_read(bi, "scalar")
                        if j == 0 and ch == 0:
                            B.wait("scalar", spill_all)
                        mev = B.a(lambda e, o=rsl, i=pm[bi][:, :]: e.copy(o, i))
                    else:
                        bank_read(bi, "vector")
                        B.v(lambda e, o=rsl, a_=rsl, b_=pm[bi][:, :]:
                            e.tensor_add(o, a_, b_))
                        mev = B.v(lambda e, o=rsl, a_=rsl,
                                  b_=gbc[:, ch * CW:(ch + 1) * CW]:
                                  e.tensor_mul(o, a_, b_))
                    bank_mark_read(bi, mev)
                    res_sh_mil[j][ch] = mev
            wfree(sdp)
        down_done_mil[0] = B.mark_t()
        if stop == "ph5b" and H == 0:
            B.wait("sync", res_sh_mil[KT - 1][CH - 1])
            B.dma("sync", dbg["RES"][:, :], RES[:, :], "sOUT")
            return finish()

    if stop == "ph5":
        B.wait("sync", res_sh_mil[KT - 1][CH - 1])
        B.dma("sync", dbg["RES"][:, :], RES[:, :], "sOUT")
        return finish()

    res_ex_mil = [[None] * CH for _ in range(KT)]
    for grp in range(4):
        e0, e1 = 2 * grp, 2 * grp + 1
        cbc_last = None
        for ei, ex in enumerate((e0, e1)):
            for ch in range(CH):
                bi = next_bank()
                bank_open(bi)
                if grp == 0 and ei == 0 and ch == 0:
                    B.wait("tensor", combT_mil)
                pm_w[bi] = B.t(lambda e, o=pm[bi][:, :],
                               l=bc8[:, ex * 128:(ex + 1) * 128],
                               r=combT[0:8, ch * CW:(ch + 1) * CW]:
                               e.matmul(o, l, r, start=True, stop=True))
                bank_read(bi, "vector")
                mcb = B.v(lambda e, o=cbc[ei][:, ch * CW:(ch + 1) * CW], i=pm[bi][:, :]:
                          e.tensor_copy(o, i))
                bank_mark_read(bi, mcb)
                cbc_last = mcb
        for ei, ex in enumerate((e0, e1)):
            for pb in range(4):
                sgp, cgp, vgp = wload(WGT[ex][:, pb * 256:(pb + 1) * 256], KT, 256)
                sup, cup, vup = wload(WUT[ex][:, pb * 256:(pb + 1) * 256], KT, 256)
                for jj in range(2):
                    it = ei * 8 + pb * 2 + jj
                    umils = []
                    gpair = [(0, next_bank()), (1, next_bank())]
                    for ch, bi in gpair:
                        bank_open(bi)
                        if jj == 0 and ch == 0:
                            wready(sgp, cgp)
                        proj_mm(vgp, jj, ch, bi, rhs3=x1b3)
                        bank_read(bi, "scalar")
                        if down_done_mil[0] is not None:
                            B.wait("scalar", down_done_mil[0])
                        mgs = B.a(lambda e, o=hb3[:, it, ch * CW:(ch + 1) * CW],
                                  i=pm[bi][:, :]:
                                  e.activation(o, i, AF.Silu))
                        bank_mark_read(bi, mgs)
                        umils.append(mgs)
                    if jj == 1:
                        wfree(sgp)
                    upair = [(0, next_bank()), (1, next_bank())]
                    for ch, bi in upair:
                        bank_open(bi)
                        if jj == 0 and ch == 0:
                            wready(sup, cup)
                        proj_mm(vup, jj, ch, bi, rhs3=x1b3)
                        bank_read(bi, "scalar")
                        if uscr_free[it % 2] is not None:
                            B.wait("scalar", uscr_free[it % 2])
                        ucp = B.a(lambda e, o=uscr[it % 2][:, ch * CW:(ch + 1) * CW],
                                  i=pm[bi][:, :]:
                                  e.copy(o, i))
                        bank_mark_read(bi, ucp)
                        B.wait("vector", umils[ch])
                        B.wait("vector", ucp)
                        if down_done_mil[0] is not None:
                            B.wait("vector", down_done_mil[0])
                        B.wait("vector", cbc_last)
                        B.v(lambda e, o=hb3[:, it, ch * CW:(ch + 1) * CW],
                            a_=hb3[:, it, ch * CW:(ch + 1) * CW],
                            b_=cbc[ei][:, ch * CW:(ch + 1) * CW]:
                            e.tensor_mul(o, a_, b_))
                        mh = B.v(lambda e, o=hb3[:, it, ch * CW:(ch + 1) * CW],
                                 a_=hb3[:, it, ch * CW:(ch + 1) * CW],
                                 b_=uscr[it % 2][:, ch * CW:(ch + 1) * CW]:
                                 e.tensor_mul(o, a_, b_))
                        hc_all = mh
                        uscr_free[it % 2] = mh
                    if jj == 1:
                        wfree(sup)
        down_done_mil[0] = None
        for hb in range(4):
            sp0, cp0, vp0 = wload(WDT[e0][:, hb * 512:(hb + 1) * 512], 8, 512)
            sp1, cp1, vp1 = wload(WDT[e1][:, hb * 512:(hb + 1) * 512], 8, 512)
            for hti in range(4):
                for ch in range(CH):
                    bi = hti * 2 + ch
                    bank_open(bi)
                    if hti == 0 and ch == 0:
                        wready(sp0, cp0)
                        B.wait("tensor", hc_all)
                    pairs = [(vp0[:, k, hti * 128:(hti + 1) * 128],
                              hb3[:, k, ch * CW:(ch + 1) * CW]) for k in range(8)]
                    mm_group(bi, pairs, pm[bi][:, :], start=True, stop=False)
            wfree(sp0)
            for hti in range(4):
                j = hb * 4 + hti
                for ch in range(CH):
                    bi = hti * 2 + ch
                    if hti == 0 and ch == 0:
                        wready(sp1, cp1)
                    pairs = [(vp1[:, k, hti * 128:(hti + 1) * 128],
                              hb3[:, 8 + k, ch * CW:(ch + 1) * CW]) for k in range(8)]
                    mm_group(bi, pairs, pm[bi][:, :], start=False, stop=True)
                for ch, bi in ((0, hti * 2), (1, hti * 2 + 1)):
                    bank_read(bi, "vector")
                    rsl = RES[:, j * TT + ch * CW: j * TT + (ch + 1) * CW]
                    if grp == 0:
                        B.wait("vector", res_sh_mil[j][ch])
                    mev = B.v(lambda e, o=rsl, a_=rsl, b_=pm[bi][:, :]:
                              e.tensor_add(o, a_, b_))
                    bank_mark_read(bi, mev)
                    res_ex_mil[j][ch] = mev
            wfree(sp1)
        down_done_mil[0] = B.mark_t()

    if stop == "ph6":
        B.wait("sync", res_ex_mil[KT - 1][CH - 1])
        B.dma("sync", dbg["RES"][:, :], RES[:, :], "sOUT")
        return finish()

    acc_mil = [None] * KT
    for j in range(KT):
        B.wait("gpsimd", res_ex_mil[j][0])
        B.wait("gpsimd", res_ex_mil[j][1])
        B.wait("gpsimd", spill_mil[j])
        acc_mil[j] = B.dma("gpsimd", RES[:, j * TT:(j + 1) * TT],
                           x1sp[:, j * TT:(j + 1) * TT], "sAC", accum_op=ALU.add)
    cast2_free = [None, None]
    sq2_free = [None, None]
    last_sx = [None, None]
    last_sxx = [None, None]
    for sbi in (4, 5, 6, 7):
        bank_open(sbi)
    for j in range(KT):
        for ch in range(CH):
            rsl = RES[:, j * TT + ch * CW: j * TT + (ch + 1) * CW]
            B.wait("vector", acc_mil[j])
            if cast2_free[ch] is not None:
                B.wait("vector", cast2_free[ch])
            mxc = B.v(lambda e, o=xcast[:, ch * CW:(ch + 1) * CW], i=rsl:
                      e.tensor_copy(o, i))
            B.wait("scalar", acc_mil[j])
            if sq2_free[ch] is not None:
                B.wait("scalar", sq2_free[ch])
            msq = B.a(lambda e, o=sqcast[:, ch * CW:(ch + 1) * CW], i=rsl:
                      e.activation(o, i, AF.Square))
            B.wait("tensor", mxc)
            mS = B.t(lambda e, o=pm[4 + ch][0:1, :], l=onescb[:, :],
                     r=xcast[:, ch * CW:(ch + 1) * CW],
                     st=(j == 0), sp=(j == KT - 1):
                     e.matmul(o, l, r, start=st, stop=sp))
            last_sx[ch] = mS
            cast2_free[ch] = mS
            B.wait("tensor", msq)
            mQ = B.t(lambda e, o=pm[6 + ch][0:1, :], l=onescb[:, :],
                     r=sqcast[:, ch * CW:(ch + 1) * CW],
                     st=(j == 0), sp=(j == KT - 1):
                     e.matmul(o, l, r, start=st, stop=sp))
            last_sxx[ch] = mQ
            sq2_free[ch] = mQ
    for ch in range(CH):
        pm_w[4 + ch] = last_sx[ch]
        pm_w[6 + ch] = last_sxx[ch]
    nm_bi, rs_bi = ln_rows_and_bcast()
    for ch in range(CH):
        bank_read(rs_bi[ch], "vector")
        mrscp = B.v(lambda e, o=rssb[:, ch * CW:(ch + 1) * CW], i=pm[rs_bi[ch]][:, :]:
                    e.tensor_copy(o, i))
        bank_mark_read(rs_bi[ch], mrscp)
        bank_read(nm_bi[ch], "vector")
        mnmcp = B.v(lambda e, o=nmsb[:, ch * CW:(ch + 1) * CW], i=pm[nm_bi[ch]][:, :]:
                    e.tensor_copy(o, i))
        bank_mark_read(nm_bi[ch], mnmcp)
    for j in range(KT):
        B.v(lambda e, o=RES[:, j * TT:(j + 1) * TT],
            a_=RES[:, j * TT:(j + 1) * TT], b_=nmsb[:, :]:
            e.tensor_add(o, a_, b_))
        m3 = B.v(lambda e, o=RES[:, j * TT:(j + 1) * TT],
                 a_=RES[:, j * TT:(j + 1) * TT], b_=rssb[:, :]:
                 e.tensor_mul(o, a_, b_))
        B.wait("sync", m3)
        B.dma("sync", OUT[j * 128:(j + 1) * 128, :], RES[:, j * TT:(j + 1) * TT], "sOUT")

    return finish()



def make_inmaps(inputs):
    x = np.asarray(inputs["L0_attn_x"], dtype=np.float32)
    f32 = lambda a: np.ascontiguousarray(np.asarray(a, dtype=np.float32))
    WIN = f32(inputs["L0_attn_W_in"])
    WDEC = f32(inputs["L0_attn_W_decay"])
    WGATE = f32(inputs["L0_attn_W_gate"])
    WOUT = f32(inputs["L0_attn_W_out"])
    bsh = lambda b: np.ascontiguousarray(np.asarray(b, np.float32).reshape(KT, 128).T)
    BIN = bsh(inputs["L0_attn_b_in"])
    BDEC = bsh(inputs["L0_attn_b_decay"])
    DBIAS = bsh(inputs["L0_attn_decay_bias"])
    BGATE = bsh(inputs["L0_attn_b_gate"])
    BOUT = bsh(inputs["L0_attn_b_out"])
    wr = np.asarray(inputs["L0_ffn_w_router"], np.float32)
    WRH = np.ascontiguousarray(wr.T.reshape(KT, 128, E).transpose(1, 0, 2).reshape(128, KT * E))
    gw = np.asarray(inputs["L0_ffn_gate_w"], np.float32)
    GWH = np.ascontiguousarray(gw.reshape(KT, 128).T)
    SWG = f32(np.asarray(inputs["L0_ffn_sw_gate"]).T)
    SWU = f32(np.asarray(inputs["L0_ffn_sw_up"]).T)
    SWD = f32(np.asarray(inputs["L0_ffn_sw_down"]).T)
    WGT = f32(np.asarray(inputs["L0_ffn_wg"]).transpose(0, 2, 1))
    WUT = f32(np.asarray(inputs["L0_ffn_wu"]).transpose(0, 2, 1))
    WDT = f32(np.asarray(inputs["L0_ffn_wd"]).transpose(0, 2, 1))
    IDENT = np.eye(128, dtype=np.float32)
    CSTS = np.zeros((128, 8), np.float32)
    CSTS[:, 0] = EPS
    CSTS[:, 1] = -0.5
    CSTS[:, 2] = -1.0
    ONESR = np.ones((1, 128), np.float32)
    ONESC = np.ones((128, 1), np.float32)
    BC8 = np.zeros((E, E * 128), np.float32)
    for e in range(E):
        BC8[e, e * 128:(e + 1) * 128] = 1.0
    maps = []
    for c in range(NCORE):
        b, q = c // 4, c % 4
        XTC = np.ascontiguousarray(x[b, q * TT:(q + 1) * TT, :].T)
        SELMC = np.zeros((128, 128), np.float32)
        SELMC[:, c * 16:(c + 1) * 16] = 1.0
        maps.append(dict(
            XT=XTC, WIN=WIN, WDEC=WDEC, WGATE=WGATE, WOUT=WOUT,
            BIN=BIN, BDEC=BDEC, DBIAS=DBIAS, BGATE=BGATE, BOUT=BOUT,
            WRH=WRH, GWH=GWH, SWG=SWG, SWU=SWU, SWD=SWD,
            WGT=WGT, WUT=WUT, WDT=WDT,
            IDENT=IDENT, ONESR=ONESR, ONESC=ONESC, BC8=BC8, SELM=SELMC, CSTS=CSTS,
        ))
    return maps


def assemble(results):
    outs = []
    for b in range(2):
        cols = np.concatenate([results[4 * b + q]["OUT"] for q in range(4)], axis=1)
        outs.append(cols.T)
    return np.ascontiguousarray(np.stack(outs, axis=0).astype(np.float32))


def kernel(**inputs):
    from concourse.bass_utils import run_bass_kernel_spmd
    maps = make_inmaps(inputs)
    nc = build_nc(stop="full", debug=False)
    res = run_bass_kernel_spmd(nc, maps, core_ids=list(range(NCORE)))
    return assemble(res.results)
